# version 06
# baseline (speedup 1.0000x reference)
import contextlib
import numpy as np
import ml_dtypes
import concourse.bass as bass
import concourse.mybir as mybir
from concourse.ap import AP
from concourse.bass_core_rust import IndirectOffsetOnAxis

F32 = mybir.dt.float32
BF16 = mybir.dt.bfloat16
I32 = mybir.dt.int32
AF = mybir.ActivationFunctionType
ALU = mybir.AluOpType
AX = mybir.AxisListType
BF = ml_dtypes.bfloat16

ENGINES = ("sync", "tensor", "vector", "scalar", "gpsimd")

FULL = dict(BPC=2, L=2048, D=1024, H=16, FF=4096, E=8)
MINI = dict(BPC=2, L=256, D=256, H=4, FF=512, E=8)


class Sem:
    def __init__(self, stack, nc, name):
        self.h = stack.enter_context(nc.semaphore(name))
        self.n = 0


class Res:
    __slots__ = ("ready", "free", "name")

    def __init__(self, name=""):
        self.ready = {}
        self.free = {}
        self.name = name


def _merge(d, sem, cnt):
    if cnt > d.get(sem, -1):
        d[sem] = cnt


class Plan:
    def __init__(self, nc, sems, stack):
        self.nc = nc
        self.stack = stack
        self.items = {e: [] for e in ENGINES}
        self.sems = sems
        self.hw = {e: {} for e in ENGINES}
        self.dma_sems = set()
        self.res_sems = {}
        self.cur_dma_sem = {"sync": None, "gpsimd": None}
        self.nsem = 0
        self.nops = 0
        self.pending = {}
        self.base_extra = []

    def fence(self):
        sems = set(self.sems.values()) | self.dma_sems
        self.base_extra = [(s, s.n) for s in sems if s.n > 0]

    def res_sem(self, res, eng):
        key = (id(res), eng)
        s = self.res_sems.get(key)
        if s is None:
            s = Sem(self.stack, self.nc, f"dr{self.nsem}")
            self.nsem += 1
            self.res_sems[key] = s
            self.dma_sems.add(s)
        return s

    def step(self, eng, body, reads=(), writes=(), extra=(), inc=True,
             dma=False):
        if dma:
            if writes:
                sem = self.res_sem(writes[0], eng)
                self.cur_dma_sem[eng] = sem
            else:
                sem = self.cur_dma_sem[eng]
                assert sem is not None, "continuation DMA with no current sem"
            k = 16
        else:
            sem = self.sems[eng]
            k = 16 if eng in ("sync", "gpsimd") else 1
        waits = {}
        for r in reads:
            for s, c in r.ready.items():
                if c > waits.get(s, -1):
                    waits[s] = c
        for wb in writes:
            for d in (wb.free, wb.ready):
                for s, c in d.items():
                    if c > waits.get(s, -1):
                        waits[s] = c
        for s, c in list(extra) + self.base_extra:
            if c > waits.get(s, -1):
                waits[s] = c
        for s in list(waits):
            if s in self.dma_sems:
                waits[s] = s.n
        hww = self.hw[eng]
        wl = []
        for s, c in waits.items():
            if c > hww.get(s, -1):
                wl.append((s.h, c))
                hww[s] = c
        if inc:
            sem.n += k
            ih = sem.h
        else:
            ih = None

        def fn(e, body=body, wl=tuple(wl), ih=ih, k=k):
            for h, v in wl:
                e.wait_ge(h, v)
            ins = body(e)
            if ih is not None:
                ins.then_inc(ih, k)
        self.items[eng].append(fn)
        self.nops += 1
        if inc:
            for wb in writes:
                wb.ready = {sem: sem.n}
                wb.free = {}
            for r in reads:
                _merge(r.free, sem, sem.n)
            pr, pw = self.pending.get(eng, ((), ()))
            for r in pr:
                _merge(r.free, sem, sem.n)
            for wb in pw:
                _merge(wb.ready, sem, sem.n)
            self.pending[eng] = ([], [])
        else:
            pr, pw = self.pending.setdefault(eng, ([], []))
            pr.extend(reads)
            pw.extend(writes)
        return sem.n

    def dma(self, eng, out_ap, in_ap, reads=(), writes=(), extra=(), **kw):
        return self.step(
            eng, lambda e, o=out_ap, i=in_ap, kw=kw: e.dma_start(out=o, in_=i, **kw),
            reads=reads, writes=writes, extra=extra, dma=True)

    def replay(self):
        with self.nc.Block() as block:
            for eng in ENGINES:
                items = self.items[eng]

                def make(items):
                    def run(e):
                        for fn in items:
                            fn(e)
                    return run
                getattr(block, eng)(make(items))


def cdiv(a, b):
    return (a + b - 1) // b


def build(cfg, cut=6, acut=9):
    B, L, D, H, FF, E = (cfg[k] for k in ("BPC", "L", "D", "H", "FF", "E"))
    DH = 64
    Lr = 2 * L - 1
    SCALE = float(np.float32(1.0) / np.sqrt(np.float32(DH)))
    NT = L // 128
    NG = H // 2
    ND = D // 128
    NF = FF // 128
    STRIP = min(512, L)
    NSTRIP = L // STRIP
    QC = 2
    LCH = 512
    NJC = cdiv(L, LCH)
    NDC = cdiv(D, 512)

    nc = bass.Bass("TRN2", target_bir_lowering=False, debug=False)

    xin = nc.dram_tensor("xin", [B, L, D], F32, kind="ExternalInput")
    wqT = nc.dram_tensor("wqT", [D, D], F32, kind="ExternalInput")
    wkT = nc.dram_tensor("wkT", [D, D], F32, kind="ExternalInput")
    wvT = nc.dram_tensor("wvT", [D, D], F32, kind="ExternalInput")
    wrT = nc.dram_tensor("wrT", [D, D], F32, kind="ExternalInput")
    rT = nc.dram_tensor("rT", [D, Lr], F32, kind="ExternalInput")
    ucols_t = nc.dram_tensor("ucols", [128, NG], F32, kind="ExternalInput")
    vcols_t = nc.dram_tensor("vcols", [128, NG], F32, kind="ExternalInput")
    wrffn = nc.dram_tensor("wrffn", [D, E], F32, kind="ExternalInput")
    w1d = nc.dram_tensor("w1", [E * D, FF], F32, kind="ExternalInput")
    b1d = nc.dram_tensor("b1", [E, FF], F32, kind="ExternalInput")
    w2d = nc.dram_tensor("w2", [E * FF, D], F32, kind="ExternalInput")
    b2d = nc.dram_tensor("b2", [E, D], F32, kind="ExternalInput")
    yout = nc.dram_tensor("yout", [B, L, D], F32, kind="ExternalOutput")

    idbf_t = nc.inline_tensor(np.eye(128, dtype=BF), "idbf")
    idf_t = nc.inline_tensor(np.eye(128, dtype=np.float32), "idf32")
    onesr_t = nc.inline_tensor(np.ones((1, 128), np.float32), "onesr")
    onesc_t = nc.inline_tensor(np.ones((128, 1), np.float32), "onesc")
    iota8_t = nc.inline_tensor(np.arange(E, dtype=np.float32).reshape(1, E), "iota8")
    iotac_t = nc.inline_tensor(np.arange(128, dtype=np.float32).reshape(128, 1),
                               "iotac")

    scr_bnc = [nc.dram_tensor(f"scr_bnc{i}", [L * 2 * L], BF16, kind="Internal")
               for i in range(2)]
    scr_rp = nc.dram_tensor("scr_rp", [NG, 128, Lr], BF16, kind="Internal")
    scr_attn = nc.dram_tensor("scr_attn", [B * L * D], BF16, kind="Internal")
    scr_x2 = nc.dram_tensor("scr_x2", [B, L, D], F32, kind="Internal")
    scr_w2 = [nc.dram_tensor(f"scr_w2_{i}", [FF * D], BF16, kind="Internal")
              for i in range(2)]
    scr_ffn = nc.dram_tensor("scr_ffn", [D * L], F32, kind="Internal")

    st = contextlib.ExitStack()
    with st:
        dS = Sem(st, nc, "dS")
        dG = Sem(st, nc, "dG")
        tM = Sem(st, nc, "tM")
        vE = Sem(st, nc, "vE")
        sA = Sem(st, nc, "sA")
        P = Plan(nc, {"sync": dS, "gpsimd": dG, "tensor": tM, "vector": vE,
                      "scalar": sA}, st)

        _gregs = {}

        def greg(e, name):
            if name not in _gregs:
                _gregs[name] = e.alloc_register(name)
            return _gregs[name]

        ps_sc = [st.enter_context(nc.psum_tensor(f"psc{i}", [128, 512], F32))
                 for i in range(2)]
        ps_big = st.enter_context(nc.psum_tensor("ps_big", [128, 2048], F32))
        ps_tr = st.enter_context(nc.psum_tensor("ps_tr", [128, 1024], BF16))
        ps_av = st.enter_context(nc.psum_tensor("ps_av", [128, 512], F32))
        ps_ut = ps_av
        ps_tf = ps_big
        r_ps = [Res(f"ps{i}") for i in range(2)]
        r_big = Res("ps_big")
        r_ptr = Res("ps_tr")
        r_pav = Res("ps_av")
        r_put = Res("ps_ut")
        r_ptf = r_big

        SB_TOTAL = 194 * 1024
        SB_BASE = nc.bump_sbuf(SB_TOTAL)[0]
        assert SB_BASE + SB_TOTAL <= nc.SBUF_PARTITION_SIZE_BYTES - 4096, SB_BASE

        def rnd(x):
            return (x + 31) & ~31

        def salloc(name, cols, dtype, off):
            esz = {F32: 4, BF16: 2, I32: 4}[dtype]
            assert off + cols * esz <= SB_TOTAL, \
                f"{name}: {off + cols * esz} > {SB_TOTAL}"
            return nc.alloc_sbuf_tensor_at(name, [128, cols], dtype,
                                           offset=SB_BASE + off).ap()

        class Reg:
            def __init__(self, base):
                self.o = [base]

            def al(self, name, cols, dtype):
                esz = {F32: 4, BF16: 2, I32: 4}[dtype]
                ap = salloc(name, cols, dtype, self.o[0])
                self.o[0] = rnd(self.o[0] + cols * esz)
                return ap

        PR = Reg(0)
        u_sb = PR.al("u_sb", NG, F32)
        v_sb = PR.al("v_sb", NG, F32)
        eps_sb = PR.al("eps_sb", 1, F32)
        ebias = PR.al("ebias", 1, F32)
        id_bf = PR.al("id_bf", 128, BF16)
        id_f = PR.al("id_f", 128, F32)
        ones_r = PR.al("ones_r", 128, F32)
        ones_c = PR.al("ones_c", 1, F32)
        iota_r = PR.al("iota_r", E, F32)
        iota_c = PR.al("iota_c", 1, F32)
        ssl = [PR.al(f"ssl{i}", NT, F32) for i in range(2)]
        rsl = [PR.al(f"rsl{i}", NT, F32) for i in range(2)]
        gcol = PR.al("gcol", 2, F32)
        ecol = PR.al("ecol", 2, F32)
        b1cols = PR.al("b1cols", NF, F32)
        b2cols = PR.al("b2cols", ND, F32)
        idx1 = [PR.al(f"idx1_{k}", ND, I32) for k in range(2)]
        idx2 = [PR.al(f"idx2_{k}", NF, I32) for k in range(2)]
        tmpc = PR.al("tmpc", 8, F32)
        b2i = PR.al("b2i", 2, I32)
        x2T = PR.al("x2T", ND * L, BF16)
        PERS_END = PR.o[0]

        AR = Reg(rnd(PERS_END))
        xT = AR.al("xT", ND * L, BF16)
        quT = AR.al("quT", L, BF16)
        qvT = AR.al("qvT", L, BF16)
        kTb = AR.al("kTb", L, BF16)
        vTp = AR.al("vTp", L, BF16)
        vsl = AR.al("vsl", NT * 128, BF16)
        rp_pair = [AR.al(f"rp_p{i}", Lr, BF16) for i in range(2)]
        w_sl = AR.al("w_sl", 2 * ND * 128, BF16)
        rawb = [[AR.al(f"raw{h}_{i}", 2304, BF16) for i in range(2)]
                for h in range(2)]
        bdsh = [[AR.al(f"bdsh{h}_{i}", L, BF16) for i in range(2)]
                for h in range(2)]
        ptile = [AR.al(f"pt{h}", L, BF16) for h in range(2)]
        ptT = [AR.al(f"ptT{h}", QC * NT * 128, BF16) for h in range(2)]
        otp = AR.al("otp", L, F32)
        otp2 = AR.al("otp2", L, F32)
        att_sb = AR.al("att_sb", 128, BF16)
        xb_sb = [AR.al(f"xb{i}", D, BF16) for i in range(2)]
        ATTN_END = AR.o[0]

        FR = Reg(rnd(PERS_END))
        w1b = FR.al("w1b", ND * FF, BF16)
        hT_off = FR.o[0]
        hT = FR.al("hT", NF * STRIP, BF16)
        HTOFF = rnd(PERS_END)
        w2c = [FR.al(f"w2c{i}", NF * 128, BF16) for i in range(2)]
        x_f = [FR.al(f"x_f{i}", D, F32) for i in range(2)]
        at_f = [FR.al(f"at_f{i}", D, BF16) for i in range(2)]
        t_f = FR.al("t_f", D, F32)
        x2_f = FR.al("x2_f", D, F32)
        x2b_f = FR.al("x2b_f", D, BF16)
        stat = FR.al("stat", 16, F32)
        pool_r = FR.al("pool_r", D, F32)
        wrf_sb = FR.al("wrf_sb", E, F32)
        rout = FR.al("rout", 64, F32)
        yev = FR.al("yev", STRIP, F32)
        yT_f = FR.al("yT_f", D, F32)
        fn_f = FR.al("fn_f", D, F32)
        t2_f = FR.al("t2_f", D, F32)
        o_f = FR.al("o_f", D, F32)
        need = FF * 4 + 2 * D * 4 + D * 2 + D * 4
        if NF * STRIP * 2 >= need:
            b1row = salloc("b1row", FF, F32, hT_off)
            b2row = [salloc(f"b2row{i}", D, F32, hT_off + FF * 4 + i * D * 4)
                     for i in range(2)]
            w2stage = salloc("w2stage", D, BF16, hT_off + FF * 4 + 2 * D * 4)
            sq_f = salloc("sq_f", D, F32,
                          hT_off + FF * 4 + 2 * D * 4 + D * 2)
        else:
            b1row = FR.al("b1row", FF, F32)
            b2row = [FR.al(f"b2row{i}", D, F32) for i in range(2)]
            w2stage = FR.al("w2stage", D, BF16)
            sq_f = FR.al("sq_f", D, F32)
        FFN_END = FR.o[0]
        assert ATTN_END <= SB_TOTAL and FFN_END <= SB_TOTAL, \
            (PERS_END, ATTN_END, FFN_END, SB_TOTAL)

        r_consts = Res("consts")
        r_x2T = Res("x2T")
        r_ssl = [Res("ssl0"), Res("ssl1")]
        r_rsl = [Res("rsl0"), Res("rsl1")]
        r_gcol = Res("gcol")
        r_ecol = Res("ecol")
        r_b1c = Res("b1c")
        r_b2c = Res("b2c")
        r_idx1 = [Res("idx1a"), Res("idx1b")]
        r_idx2 = [Res("idx2a"), Res("idx2b")]
        r_tmpc = Res("tmpc")
        r_b2i = Res("b2i")
        r_xT = Res("xT")
        r_qu = Res("qu")
        r_qv = Res("qv")
        r_kT = Res("kT")
        r_vTp = Res("vTp")
        r_vsl = Res("vsl")
        r_rpp = [Res("rp0"), Res("rp1")]
        r_wsl = [Res("w_sl0"), Res("w_sl1")]
        r_raw = [[Res(f"raw{h}{i}") for i in range(2)] for h in range(2)]
        r_bdsh = [[Res(f"bdsh{h}{i}") for i in range(2)] for h in range(2)]
        r_pt = [Res("pt0"), Res("pt1")]
        r_ptT = [Res("ptT0"), Res("ptT1")]
        r_otp = Res("otp")
        r_att = Res("att")
        r_xb = [Res("xb0"), Res("xb1")]
        r_ring = [Res("ring0"), Res("ring1")]
        r_scr_rp = Res("scr_rp")
        r_scra = Res("scra")
        r_scrx2 = Res("scrx2")
        r_scrw2 = [Res("scrw2a"), Res("scrw2b")]
        r_scrf = Res("scrf")
        r_xf = [Res("xf0"), Res("xf1")]
        r_atf = [Res("atf0"), Res("atf1")]
        r_tf = Res("tf")
        r_x2f = Res("x2f")
        r_x2b = Res("x2b")
        r_stat = Res("stat")
        r_pool = Res("pool")
        r_wrf = Res("wrf")
        r_rout = Res("rout")
        r_w1b = Res("w1b")
        r_hT = Res("hT")
        r_w2c = [Res("w2c0"), Res("w2c1")]
        r_yev = Res("yev")
        r_yT = Res("yT")
        r_fn = Res("fn")
        r_t2 = Res("t2")
        r_of = Res("of")
        r_sq = r_hT
        r_b1r = r_hT
        r_b2r = [r_hT, r_hT]
        r_w2s = r_hT
        r_y = Res("yout")

        xT3 = xT.rearrange("p (d l) -> p d l", d=ND)
        x2T3 = x2T.rearrange("p (d l) -> p d l", d=ND)
        w1b3 = w1b.rearrange("p (d f) -> p d f", d=ND)
        hT3 = hT.rearrange("p (f s) -> p f s", f=NF)
        w2c3 = [w2c[i].rearrange("p (f c) -> p f c", f=NF) for i in range(2)]
        ptT4 = [ptT[h].rearrange("p (a j i) -> p a j i", a=QC, j=NT)
                for h in range(2)]
        yT3 = yT_f.rearrange("p (d c) -> p d c", d=ND)

        for dst, t in ((u_sb, ucols_t), (v_sb, vcols_t), (id_bf, idbf_t),
                       (id_f, idf_t), (ones_c, onesc_t), (iota_c, iotac_t)):
            P.dma("sync", dst, t.ap()[:, :], writes=[r_consts])
        P.dma("sync", ones_r[0:1, :], onesr_t.ap()[:, :], writes=[r_consts])
        P.dma("sync", iota_r[0:1, :], iota8_t.ap()[:, :], writes=[r_consts])
        P.step("vector", lambda e: e.memset(eps_sb, 1e-5), reads=[r_consts])
        P.step("vector", lambda e: e.memset(ebias, -12.0), reads=[r_consts])
        _merge(r_consts.ready, vE, vE.n)
        CE = list(r_consts.ready.items())

        RP = Reg(rnd(PERS_END))
        rtb = [RP.al(f"rtb{dt}", Lr, BF16) for dt in range(ND)]
        wr_sl = RP.al("wr_sl", ND * 128, BF16)
        rp_out = RP.al("rp_out", Lr, BF16)
        assert RP.o[0] <= SB_TOTAL
        r_rtb = [Res(f"rtb{dt}") for dt in range(ND)]
        r_wr = Res("wr")
        r_rpo = Res("rpo")

        for dt in range(ND):
            P.dma("gpsimd", rtb[dt], rT.ap()[dt * 128:(dt + 1) * 128, :],
                  writes=[r_rtb[dt]], extra=CE)
        NRC = cdiv(Lr, 512)
        for g in range(NG):
            for dt in range(ND):
                P.dma("gpsimd", wr_sl[:, dt * 128:(dt + 1) * 128],
                      wrT.ap()[dt * 128:(dt + 1) * 128, g * 128:(g + 1) * 128],
                      writes=[r_wr])
            for rc in range(NRC):
                w = min(512, Lr - rc * 512)
                sl = rc % 2
                for dt in range(ND):
                    P.step("tensor",
                           lambda e, ps=ps_sc[sl], dt=dt, rc=rc, w=w:
                           e.matmul(ps[:, 0:w], wr_sl[:, dt * 128:(dt + 1) * 128],
                                    rtb[dt][:, rc * 512:rc * 512 + w],
                                    start=(dt == 0), stop=(dt == ND - 1)),
                           reads=[r_wr, r_rtb[dt]],
                           writes=[r_ps[sl]] if dt == 0 else (),
                           inc=(dt == ND - 1))
                _merge(r_ps[sl].ready, tM, tM.n)
                P.step("vector",
                       lambda e, ps=ps_sc[sl], rc=rc, w=w:
                       e.tensor_copy(rp_out[:, rc * 512:rc * 512 + w], ps[:, 0:w]),
                       reads=[r_ps[sl]], writes=[r_rpo] if rc == 0 else ())
            _merge(r_rpo.ready, vE, vE.n)
            P.dma("sync", scr_rp.ap()[g, :, :], rp_out, reads=[r_rpo],
                  writes=[r_scr_rp])
        P.fence()

        def dummy_out():
            P.step("vector", lambda e: e.memset(o_f, 0.0), writes=[r_of])
            for bb in range(B):
                for kb in range(NT):
                    P.dma("sync", yout.ap()[bb, kb * 128:(kb + 1) * 128, :],
                          o_f, reads=[r_of], writes=[r_y])

        for b in range(B if cut >= 2 else 0):
            for stp in range(NT):
                xb, rxb = xb_sb[stp % 2], r_xb[stp % 2]
                P.dma("gpsimd", xb, xin.ap()[b, stp * 128:(stp + 1) * 128, :],
                      writes=[rxb], extra=CE)
                for lo in range(0, ND, 8):
                    n = min(8, ND - lo)
                    for j in range(n):
                        P.step("tensor",
                               lambda e, xb=xb, dt=lo + j, j=j:
                               e.transpose(ps_tr[:, j * 128:(j + 1) * 128],
                                           xb[:, dt * 128:(dt + 1) * 128], id_bf),
                               reads=[rxb, r_consts],
                               writes=[r_ptr] if j == 0 else (),
                               inc=(j == n - 1))
                    _merge(r_ptr.ready, tM, tM.n)
                    P.step("vector",
                           lambda e, lo=lo, n=n, stp=stp:
                           e.tensor_copy(xT3[:, lo:lo + n,
                                             stp * 128:(stp + 1) * 128],
                                         ps_tr[:, 0:n * 128]),
                           reads=[r_ptr],
                           writes=[r_xT] if (stp == 0 and lo == 0) else ())
            _merge(r_xT.ready, vE, vE.n)

            for g in range(NG if cut >= 3 else 0):
                hseq = b * NG + g
                ring = scr_bnc[hseq % 2]
                rring = r_ring[hseq % 2]
                rp_b, r_rp = rp_pair[g % 2], r_rpp[g % 2]
                P.dma("sync", rp_b, scr_rp.ap()[g, :, :],
                      reads=[r_scr_rp], writes=[r_rp])

                def proj(wdram, outs, rress, biases, wi):
                    wsl_b = w_sl[:, wi * ND * 128:(wi + 1) * ND * 128]
                    r_w = r_wsl[wi]
                    for dt in range(ND):
                        P.dma("gpsimd", wsl_b[:, dt * 128:(dt + 1) * 128],
                              wdram.ap()[dt * 128:(dt + 1) * 128,
                                         g * 128:(g + 1) * 128],
                              writes=[r_w])
                    for fc in range(cdiv(L, 512)):
                        w = min(512, L - fc * 512)
                        sl = fc % 4
                        poff = 512 * sl
                        for dt in range(ND):
                            P.step("tensor",
                                   lambda e, poff=poff, dt=dt, fc=fc, w=w,
                                   wsl_b=wsl_b:
                                   e.matmul(ps_big[:, poff:poff + w],
                                            wsl_b[:, dt * 128:(dt + 1) * 128],
                                            xT3[:, dt, fc * 512:fc * 512 + w],
                                            start=(dt == 0), stop=(dt == ND - 1)),
                                   reads=[r_w, r_xT],
                                   writes=[r_big] if dt == 0 else (),
                                   inc=(dt == ND - 1))
                        _merge(r_big.ready, tM, tM.n)
                        for ob, rr, bias in zip(outs, rress, biases):
                            if bias is None:
                                P.step("scalar",
                                       lambda e, poff=poff, fc=fc, ob=ob, w=w:
                                       e.activation(ob[:, fc * 512:fc * 512 + w],
                                                    ps_big[:, poff:poff + w],
                                                    AF.Copy),
                                       reads=[r_big],
                                       writes=[rr] if fc == 0 else ())
                            else:
                                P.step("vector",
                                       lambda e, poff=poff, fc=fc, ob=ob,
                                       bias=bias, w=w:
                                       e.tensor_scalar(
                                           ob[:, fc * 512:fc * 512 + w],
                                           ps_big[:, poff:poff + w],
                                           bias, None, ALU.add),
                                       reads=[r_big, r_consts],
                                       writes=[rr] if fc == 0 else ())
                    for rr in rress:
                        _merge(rr.ready, sA, sA.n)
                        _merge(rr.ready, vE, vE.n)

                proj(wqT, [quT, qvT], [r_qu, r_qv],
                     [u_sb[:, g:g + 1], v_sb[:, g:g + 1]], 0)
                proj(wkT, [kTb], [r_kT], [None], 1)
                proj(wvT, [vTp], [r_vTp], [None], 0)
                for lo in range(0, NT, 8):
                    n = min(8, NT - lo)
                    for j in range(n):
                        P.step("tensor",
                               lambda e, jt=lo + j, j=j:
                               e.transpose(ps_tr[:, j * 128:(j + 1) * 128],
                                           vTp[:, jt * 128:(jt + 1) * 128], id_bf),
                               reads=[r_vTp, r_consts],
                               writes=[r_ptr] if j == 0 else (),
                               inc=(j == n - 1))
                    _merge(r_ptr.ready, tM, tM.n)
                    P.step("vector",
                           lambda e, lo=lo, n=n:
                           e.tensor_copy(vsl[:, lo * 128:(lo + n) * 128],
                                         ps_tr[:, 0:n * 128]),
                           reads=[r_ptr],
                           writes=[r_vsl] if lo == 0 else ())
                _merge(r_vsl.ready, vE, vE.n)

                def windows(t):
                    i0 = t * 128
                    hw_ = L - 1 - i0
                    tw_ = i0 + 128
                    wins = []
                    c = 0
                    while c < hw_:
                        w = min(512, hw_ - c)
                        wins.append((1 + c, c, w))
                        c += w
                    c = 0
                    while c < tw_:
                        w = min(512, tw_ - c)
                        wins.append((1 + hw_ + c, Lr - tw_ + c, w))
                        c += w
                    return wins, hw_, tw_

                def bd_memset(t):
                    for hh in range(2):
                        P.step("vector",
                               lambda e, rb=rawb[hh][t % 2]:
                               e.memset(rb[:, 0:1], 0.0),
                               writes=[r_raw[hh][t % 2]])

                def bd_chunk(t, win):
                    doff, rpc, w = win
                    rbs = [rawb[0][t % 2], rawb[1][t % 2]]
                    rrs = [r_raw[0][t % 2], r_raw[1][t % 2]]
                    P.step("tensor",
                           lambda e, rpc=rpc, w=w, t=t, rp=rp_b:
                           e.matmul(ps_sc[0][:, 0:w],
                                    qvT[0:64, t * 128:(t + 1) * 128],
                                    rp[0:64, rpc:rpc + w],
                                    start=True, stop=True),
                           reads=[r_qv, r_rp],
                           writes=[r_ps[0]], inc=False)
                    P.step("tensor",
                           lambda e, rpc=rpc, w=w, t=t, rp=rp_b:
                           e.matmul(ps_sc[1][:, 0:w],
                                    qvT[64:128, t * 128:(t + 1) * 128],
                                    rp[64:128, rpc:rpc + w],
                                    start=True, stop=True),
                           reads=[r_qv, r_rp],
                           writes=[r_ps[1]], inc=True)
                    _merge(r_ps[0].ready, tM, tM.n)
                    _merge(r_ps[1].ready, tM, tM.n)
                    for hh in range(2):
                        P.step("vector",
                               lambda e, ps=ps_sc[hh], rb=rbs[hh], doff=doff,
                               w=w:
                               e.tensor_copy(rb[:, doff:doff + w], ps[:, 0:w]),
                               reads=[r_ps[hh]], writes=[rrs[hh]])
                        _merge(r_ps[hh].free, vE, vE.n)

                def bd_writes(t):
                    i0 = t * 128
                    _, hw_, tw_ = windows(t)
                    for hh in range(2):
                        rb, rr = rawb[hh][t % 2], r_raw[hh][t % 2]
                        P.dma("sync",
                              AP(ring, i0 * 2 * L, [[2 * L, 128], [1, hw_ + 1]]),
                              rb[:, 0:hw_ + 1], reads=[rr], writes=[rring])
                        P.dma("sync",
                              AP(ring, i0 * 2 * L + (2 * L - tw_),
                                 [[2 * L, 128], [1, tw_]]),
                              rb[:, hw_ + 1:hw_ + 1 + tw_], reads=[rr],
                              writes=[rring])

                def sc_reads(t):
                    i0 = t * 128
                    for hh in range(2):
                        src = AP(ring, i0 * (2 * L - 1) + (2 * L - 1),
                                 [[2 * L - 1, 128], [1, L]])
                        P.dma("sync", bdsh[hh][t % 2], src, reads=[rring],
                              writes=[r_bdsh[hh][t % 2]])

                def sc_chunk(t, jc, emit_tr=True):
                    w = min(LCH, L - jc * LCH)
                    co = 512 * (jc % 2)
                    P.step("tensor",
                           lambda e, co=co, jc=jc, w=w, t=t:
                           e.matmul(ps_big[:, co:co + w],
                                    quT[0:64, t * 128:(t + 1) * 128],
                                    kTb[0:64, jc * LCH:jc * LCH + w],
                                    start=True, stop=False),
                           reads=[r_qu, r_kT],
                           writes=[r_big], inc=False)
                    P.step("tensor",
                           lambda e, co=co, jc=jc, w=w, t=t:
                           e.matmul(ps_big[:, 1024 + co:1024 + co + w],
                                    quT[64:128, t * 128:(t + 1) * 128],
                                    kTb[64:128, jc * LCH:jc * LCH + w],
                                    start=True, stop=False),
                           reads=[r_qu, r_kT],
                           writes=[r_big], inc=False)
                    for hh in range(2):
                        P.step("tensor",
                               lambda e, co=co, hh=hh, jc=jc, w=w, t=t:
                               e.matmul(ps_big[:, hh * 1024 + co:
                                               hh * 1024 + co + w], id_bf,
                                        bdsh[hh][t % 2][:,
                                                        jc * LCH:jc * LCH + w],
                                        start=False, stop=True),
                               reads=[r_bdsh[hh][t % 2], r_consts],
                               inc=(hh == 1))
                    _merge(r_big.ready, tM, tM.n)
                    if jc % 2 == 1 or jc == NJC - 1:
                        p0 = (jc // 2) * 2
                        pw = sum(min(LCH, L - c * LCH)
                                 for c in range(p0, jc + 1))
                        for hh in range(2):
                            acc = (ssl[hh][:, t:t + 1] if p0 == 0
                                   else tmpc[:, 4 + hh:5 + hh])
                            racc = r_ssl[hh] if p0 == 0 else r_tmpc
                            P.step("scalar",
                                   lambda e, hh=hh, p0=p0, pw=pw, t=t, acc=acc:
                                   e.activation(
                                       ptile[hh][:, p0 * LCH:p0 * LCH + pw],
                                       ps_big[:, hh * 1024:hh * 1024 + pw],
                                       AF.Exp, bias=ebias, scale=SCALE,
                                       accum_out=acc),
                                   reads=[r_big, r_consts] +
                                         ([r_ssl[hh]] if p0 == 0 else [r_tmpc]),
                                   writes=[r_pt[hh], racc])
                            if p0 > 0:
                                P.step("vector",
                                       lambda e, hh=hh, t=t:
                                       e.tensor_add(ssl[hh][:, t:t + 1],
                                                    ssl[hh][:, t:t + 1],
                                                    tmpc[:, 4 + hh:5 + hh]),
                                       reads=[r_tmpc, r_ssl[hh]],
                                       writes=[r_ssl[hh]])
                        _merge(r_big.free, sA, sA.n)

                def sc_tr(t, jc):
                    if acut < 4:
                        return
                    w = min(LCH, L - jc * LCH)
                    njt = (w + 127) // 128
                    for hh in range(2):
                        for jj in range(njt):
                            P.step("tensor",
                                   lambda e, hh=hh, jt=jc * 4 + jj, jj=jj:
                                   e.transpose(
                                       ps_tr[:, (hh * 4 + jj) * 128:
                                             (hh * 4 + jj + 1) * 128],
                                       ptile[hh][:, jt * 128:(jt + 1) * 128],
                                       id_bf),
                                   reads=[r_pt[hh], r_consts],
                                   writes=[r_ptr] if jj == 0 else (),
                                   inc=(jj == njt - 1))
                        _merge(r_ptr.ready, tM, tM.n)
                        P.step("vector",
                               lambda e, hh=hh, jc=jc, njt=njt, tq=t % QC:
                               e.tensor_copy(
                                   ptT4[hh][:, tq, jc * 4:jc * 4 + njt, :],
                                   ps_tr[:, hh * 512:hh * 512 + njt * 128]),
                               reads=[r_ptr],
                               writes=[r_ptT[hh]]
                               if (t % QC == 0 and jc == 0) else ())
                        _merge(r_ptT[hh].ready, vE, vE.n)
                        _merge(r_pt[hh].free, vE, vE.n)
                        _merge(r_ptr.free, vE, vE.n)

                def av_emit(t):
                    if acut < 5:
                        return
                    q = t // QC
                    for hh in range(2):
                        _merge(r_ptT[hh].ready, vE, vE.n)
                    for hh, dstb in ((0, otp), (1, otp2)):
                        for jt in range(NT):
                            P.step("tensor",
                                   lambda e, hh=hh, jt=jt:
                                   e.matmul(ps_av[0:64, 0:QC * 128],
                                            vsl[:, jt * 128 + hh * 64:
                                                jt * 128 + hh * 64 + 64],
                                            ptT4[hh][:, :, jt, :],
                                            start=(jt == 0),
                                            stop=(jt == NT - 1)),
                                   reads=[r_ptT[hh], r_vsl],
                                   writes=[r_pav] if jt == 0 else (),
                                   inc=(jt == NT - 1))
                        _merge(r_pav.ready, tM, tM.n)
                        P.step("vector",
                               lambda e, q=q, dstb=dstb:
                               e.tensor_copy(dstb[0:64, q * QC * 128:
                                                  (q + 1) * QC * 128],
                                             ps_av[0:64, 0:QC * 128]),
                               reads=[r_pav])
                        _merge(r_pav.free, vE, vE.n)
                        _merge(r_ptT[hh].free, vE, vE.n)

                if acut >= 2:
                    for tt in (0, 1):
                        if tt < NT:
                            bd_memset(tt)
                            for win in windows(tt)[0]:
                                bd_chunk(tt, win)
                            bd_writes(tt)
                    if acut >= 3:
                        sc_reads(0)
                    for t in range(NT):
                        if acut < 3:
                            continue
                        tb = t + 2
                        bwins = windows(tb)[0] if tb < NT else []
                        if tb < NT:
                            bd_memset(tb)
                        nc_ = max(NJC, len(bwins))
                        for c in range(nc_):
                            if c < len(bwins):
                                bd_chunk(tb, bwins[c])
                            if c < NJC:
                                sc_chunk(t, c, emit_tr=False)
                            if 2 <= c < NJC + 2:
                                sc_tr(t, c - 2)
                        if tb < NT:
                            bd_writes(tb)
                        if t + 1 < NT:
                            sc_reads(t + 1)
                        for cc in range(max(0, min(nc_ - 2, NJC)), NJC):
                            sc_tr(t, cc)
                        if t % QC == QC - 1:
                            av_emit(t)
                _merge(r_otp.ready, vE, vE.n)

                if acut < 9:
                    continue
                for hh in range(2):
                    P.step("vector",
                           lambda e, hh=hh: e.reciprocal(rsl[hh], ssl[hh]),
                           reads=[r_ssl[hh]], writes=[r_rsl[hh]])
                for k in range(NT):
                    for hh in range(2):
                        P.step("tensor",
                               lambda e, hh=hh, k=k, src_=None:
                               e.transpose(ps_tf[:, hh * 64:hh * 64 + 64],
                                           (otp if hh == 0 else otp2)
                                           [0:64, k * 128:(k + 1) * 128],
                                           id_f[0:64, 0:64]),
                               reads=[r_otp, r_consts],
                               writes=[r_ptf] if hh == 0 else (),
                               inc=(hh == 1))
                    _merge(r_ptf.ready, tM, tM.n)
                    for hh in range(2):
                        P.step("vector",
                               lambda e, hh=hh, k=k:
                               e.tensor_scalar(att_sb[:, hh * 64:hh * 64 + 64],
                                               ps_tf[:, hh * 64:hh * 64 + 64],
                                               rsl[hh][:, k:k + 1], None,
                                               ALU.mult),
                               reads=[r_ptf, r_rsl[hh]],
                               writes=[r_att] if hh == 0 else ())
                    _merge(r_att.ready, vE, vE.n)
                    P.dma("sync",
                          AP(scr_attn, (b * L + k * 128) * D + g * 128,
                             [[D, 128], [1, 128]]),
                          att_sb, reads=[r_att], writes=[r_scra])

            P.fence()
            if cut < 4:
                continue

            for stp in range(NT):
                xf, rxf = x_f[stp % 2], r_xf[stp % 2]
                af, raf = at_f[stp % 2], r_atf[stp % 2]
                P.dma("sync", xf, xin.ap()[b, stp * 128:(stp + 1) * 128, :],
                      writes=[rxf], extra=CE)
                P.dma("sync", af,
                      AP(scr_attn, (b * L + stp * 128) * D, [[D, 128], [1, D]]),
                      reads=[r_scra], writes=[raf])
                P.step("vector", lambda e, xf=xf, af=af: e.tensor_add(t_f, xf, af),
                       reads=[rxf, raf], writes=[r_tf])
                P.step("scalar",
                       lambda e: e.activation(sq_f, t_f, AF.Square,
                                              accum_out=stat[:, 1:2]),
                       reads=[r_tf, r_stat], writes=[r_sq, r_stat])
                P.step("vector",
                       lambda e: e.reduce_sum(stat[:, 0:1], t_f, axis=AX.X),
                       reads=[r_tf, r_stat], writes=[r_stat])
                P.step("vector",
                       lambda e: e.tensor_scalar(stat[:, 2:3], stat[:, 0:1],
                                                 1.0 / D, None, ALU.mult),
                       reads=[r_stat], writes=[r_stat])
                P.step("vector",
                       lambda e: e.scalar_tensor_tensor(stat[:, 4:5],
                                                        stat[:, 2:3], 1.0,
                                                        stat[:, 2:3],
                                                        ALU.mult, ALU.mult),
                       reads=[r_stat], writes=[r_stat])
                P.step("vector",
                       lambda e: e.scalar_tensor_tensor(stat[:, 5:6],
                                                        stat[:, 1:2], 1.0 / D,
                                                        stat[:, 4:5],
                                                        ALU.mult, ALU.subtract),
                       reads=[r_stat], writes=[r_stat])
                P.step("scalar",
                       lambda e: e.activation(stat[:, 6:7], stat[:, 5:6],
                                              AF.Sqrt, bias=eps_sb),
                       reads=[r_stat, r_consts], writes=[r_stat])
                P.step("vector",
                       lambda e: e.reciprocal(stat[:, 7:8], stat[:, 6:7]),
                       reads=[r_stat], writes=[r_stat])
                P.step("vector",
                       lambda e: e.tensor_scalar(x2_f, t_f, stat[:, 2:3],
                                                 stat[:, 7:8], ALU.subtract,
                                                 ALU.mult),
                       reads=[r_tf, r_stat], writes=[r_x2f])
                P.dma("sync", scr_x2.ap()[b, stp * 128:(stp + 1) * 128, :], x2_f,
                      reads=[r_x2f], writes=[r_scrx2])
                for dc in range(NDC):
                    w = min(512, D - dc * 512)
                    pps = (ps_ut, ps_tf)[dc]
                    rps = (r_put, r_ptf)[dc]
                    P.step("tensor",
                           lambda e, pps=pps, dc=dc, w=w, stp=stp:
                           e.matmul(pps[0:1, 0:w], ones_c,
                                    x2_f[:, dc * 512:dc * 512 + w],
                                    start=(stp == 0), stop=(stp == NT - 1),
                                    skip_group_check=True),
                           reads=[r_x2f, r_consts],
                           writes=[rps] if stp == 0 else (),
                           inc=(dc == NDC - 1))
                    if stp == NT - 1:
                        _merge(rps.ready, tM, tM.n)
                _merge(r_x2f.free, tM, tM.n)
                P.step("vector", lambda e: e.tensor_copy(x2b_f, x2_f),
                       reads=[r_x2f], writes=[r_x2b])
                for lo in range(0, ND, 8):
                    n = min(8, ND - lo)
                    for j in range(n):
                        P.step("tensor",
                               lambda e, dt=lo + j, j=j:
                               e.transpose(ps_tr[:, j * 128:(j + 1) * 128],
                                           x2b_f[:, dt * 128:(dt + 1) * 128],
                                           id_bf),
                               reads=[r_x2b, r_consts],
                               writes=[r_ptr] if j == 0 else (),
                               inc=(j == n - 1))
                    _merge(r_ptr.ready, tM, tM.n)
                    P.step("vector",
                           lambda e, lo=lo, n=n, stp=stp:
                           e.tensor_copy(x2T3[:, lo:lo + n,
                                             stp * 128:(stp + 1) * 128],
                                         ps_tr[:, 0:n * 128]),
                           reads=[r_ptr],
                           writes=[r_x2T] if (stp == 0 and lo == 0) else ())
            _merge(r_x2T.ready, vE, vE.n)

            for dc in range(NDC):
                w = min(512, D - dc * 512)
                pps = (ps_ut, ps_tf)[dc]
                rps = (r_put, r_ptf)[dc]
                P.step("vector",
                       lambda e, pps=pps, dc=dc, w=w:
                       e.tensor_scalar(pool_r[0:1, dc * 512:dc * 512 + w],
                                       pps[0:1, 0:w], 1.0 / L, None, ALU.mult),
                       reads=[rps], writes=[r_pool] if dc == 0 else ())
            _merge(r_pool.ready, vE, vE.n)
            for dt in range(ND):
                P.step("tensor",
                       lambda e, dt=dt:
                       e.matmul(ps_ut[:, dt:dt + 1],
                                pool_r[0:1, dt * 128:(dt + 1) * 128],
                                ones_c[0:1, 0:1], start=True, stop=True,
                                skip_group_check=True),
                       reads=[r_pool, r_consts],
                       writes=[r_put] if dt == 0 else (),
                       inc=(dt == ND - 1))
            _merge(r_put.ready, tM, tM.n)
            P.step("vector",
                   lambda e: e.tensor_copy(pool_r[:, 0:ND], ps_ut[:, 0:ND]),
                   reads=[r_put], writes=[r_pool])
            for dt in range(ND):
                P.dma("sync", wrf_sb, wrffn.ap()[dt * 128:(dt + 1) * 128, :],
                      writes=[r_wrf], extra=CE)
                P.step("tensor",
                       lambda e, dt=dt:
                       e.matmul(ps_tf[0:1, 0:E], pool_r[:, dt:dt + 1],
                                wrf_sb, start=(dt == 0), stop=(dt == ND - 1),
                                skip_group_check=True),
                       reads=[r_wrf, r_pool],
                       writes=[r_ptf] if dt == 0 else (),
                       inc=True)
            _merge(r_ptf.ready, tM, tM.n)
            RO = r_rout
            P.step("vector",
                   lambda e: e.tensor_copy(rout[0:1, 0:E], ps_tf[0:1, 0:E]),
                   reads=[r_ptf], writes=[RO])
            P.step("vector",
                   lambda e: e.scalar_tensor_tensor(rout[0:1, 8:8 + E],
                                                    iota_r[0:1, :], -1e-7,
                                                    rout[0:1, 0:E],
                                                    ALU.mult, ALU.add),
                   reads=[RO, r_consts], writes=[RO])
            P.step("vector",
                   lambda e: e.reduce_max(rout[0:1, 16:17], rout[0:1, 8:8 + E],
                                          axis=AX.X),
                   reads=[RO], writes=[RO])
            P.step("vector",
                   lambda e: e.tensor_scalar(rout[0:1, 24:24 + E],
                                             rout[0:1, 8:8 + E],
                                             rout[0:1, 16:17], None, ALU.is_ge),
                   reads=[RO], writes=[RO])
            P.step("vector",
                   lambda e: e.scalar_tensor_tensor(rout[0:1, 32:32 + E],
                                                    rout[0:1, 24:24 + E], -1e30,
                                                    rout[0:1, 8:8 + E],
                                                    ALU.mult, ALU.add),
                   reads=[RO], writes=[RO])
            P.step("vector",
                   lambda e: e.reduce_max(rout[0:1, 17:18], rout[0:1, 32:32 + E],
                                          axis=AX.X),
                   reads=[RO], writes=[RO])
            P.step("vector",
                   lambda e: e.tensor_scalar(rout[0:1, 40:40 + E],
                                             rout[0:1, 32:32 + E],
                                             rout[0:1, 17:18], None, ALU.is_ge),
                   reads=[RO], writes=[RO])
            for dsti, ohoff in ((18, 24), (19, 40)):
                P.step("vector",
                       lambda e, dsti=dsti, ohoff=ohoff:
                       e.scalar_tensor_tensor(rout[0:1, 48:48 + E],
                                              rout[0:1, ohoff:ohoff + E], 1.0,
                                              rout[0:1, 0:E], ALU.mult, ALU.mult,
                                              accum_out=rout[0:1,
                                                             dsti:dsti + 1]),
                       reads=[RO], writes=[RO])
            for dsti, ohoff in ((20, 24), (21, 40)):
                P.step("vector",
                       lambda e, dsti=dsti, ohoff=ohoff:
                       e.scalar_tensor_tensor(rout[0:1, 48:48 + E],
                                              rout[0:1, ohoff:ohoff + E], 1.0,
                                              iota_r[0:1, :], ALU.mult, ALU.mult,
                                              accum_out=rout[0:1,
                                                             dsti:dsti + 1]),
                       reads=[RO, r_consts], writes=[RO])
            P.step("vector",
                   lambda e: e.tensor_sub(rout[0:1, 22:23], rout[0:1, 19:20],
                                          rout[0:1, 18:19]),
                   reads=[RO], writes=[RO])
            P.step("scalar",
                   lambda e: e.activation(rout[0:1, 23:24], rout[0:1, 22:23],
                                          AF.Sigmoid),
                   reads=[RO], writes=[RO])
            P.step("vector",
                   lambda e: e.tensor_scalar(rout[0:1, 26:27], rout[0:1, 23:24],
                                             -1.0, 1.0, ALU.mult, ALU.add),
                   reads=[RO], writes=[RO])
            for col, src in ((0, 26), (1, 23), (2, 20), (3, 21)):
                P.step("tensor",
                       lambda e, col=col, src=src:
                       e.matmul(ps_ut[:, col:col + 1], ones_r[0:1, :],
                                rout[0:1, src:src + 1], start=True, stop=True,
                                skip_group_check=True),
                       reads=[RO, r_consts],
                       writes=[r_put] if col == 0 else (),
                       inc=(col == 3))
            _merge(r_put.ready, tM, tM.n)
            P.step("vector", lambda e: e.tensor_copy(gcol, ps_ut[:, 0:2]),
                   reads=[r_put], writes=[r_gcol])
            P.step("vector", lambda e: e.tensor_copy(ecol, ps_ut[:, 2:4]),
                   reads=[r_put], writes=[r_ecol])
            P.step("vector",
                   lambda e: e.tensor_copy(b2i[0:1, 0:2], rout[0:1, 20:22]),
                   reads=[RO], writes=[r_b2i])
            for k in range(2):
                def b2gather(e, k=k):
                    r1 = greg(e, "gidx")
                    r2 = greg(e, "goff")
                    e.load(r1, b2i[0:1, k:k + 1])
                    e.reg_mul(r2, r1, D)
                    return e.dma_start(out=b2row[k][0:1, :],
                                       in_=AP(b2d, r2, [[D, 1], [1, D]]))
                P.step("gpsimd", b2gather, reads=[r_b2i], writes=[r_b2r[k]],
                       dma=True)
            P.step("vector",
                   lambda e: e.tensor_scalar(b2row[0][0:1, :], b2row[0][0:1, :],
                                             rout[0:1, 26:27], None, ALU.mult),
                   reads=[RO, r_b2r[0]], writes=[r_b2r[0]])
            P.step("vector",
                   lambda e: e.scalar_tensor_tensor(b2row[1][0:1, :],
                                                    b2row[1][0:1, :],
                                                    rout[0:1, 23:24],
                                                    b2row[0][0:1, :],
                                                    ALU.mult, ALU.add),
                   reads=[RO, r_b2r[0], r_b2r[1]], writes=[r_b2r[1]])
            for dt in range(ND):
                P.step("tensor",
                       lambda e, dt=dt:
                       e.matmul(ps_ut[:, 8 + dt:9 + dt],
                                b2row[1][0:1, dt * 128:(dt + 1) * 128],
                                ones_c[0:1, 0:1], start=True, stop=True,
                                skip_group_check=True),
                       reads=[r_b2r[1], r_consts],
                       writes=[r_put] if dt == 0 else (),
                       inc=(dt == ND - 1))
            _merge(r_put.ready, tM, tM.n)
            P.step("vector",
                   lambda e: e.tensor_copy(b2cols, ps_ut[:, 8:8 + ND]),
                   reads=[r_put], writes=[r_b2c])

            for k in range(2 if cut >= 5 else 0):
                def w1load(e, k=k):
                    r1 = greg(e, "gidx")
                    r2 = greg(e, "goff")
                    e.load(r1, b2i[0:1, k:k + 1])
                    e.reg_mul(r2, r1, D * FF)
                    src_ = AP(w1d, r2, [[FF, 128], [128 * FF, ND], [1, FF]])
                    return e.dma_start(out=w1b3[:, :, :], in_=src_)
                P.step("gpsimd", w1load, reads=[r_b2i],
                       writes=[r_w1b], dma=True)
                def w2cast(e, k=k):
                    r1 = greg(e, "gidx")
                    r2 = greg(e, "goff")
                    e.load(r1, b2i[0:1, k:k + 1])
                    e.reg_mul(r2, r1, FF * D)
                    return e.dma_start(out=AP(scr_w2[k], 0, [[1, FF * D]]),
                                       in_=AP(w2d, r2, [[1, FF * D]]))
                P.step("gpsimd", w2cast, reads=[r_b2i],
                       writes=[r_scrw2[k]], dma=True)
                P.step("vector",
                       lambda e, k=k: e.tensor_copy(b2i[0:1, 0:1],
                                                    rout[0:1, 20 + k:21 + k]),
                       reads=[RO], writes=[r_b2i])
                def b1gather(e):
                    r1 = greg(e, "gidx")
                    r2 = greg(e, "goff")
                    e.load(r1, b2i[0:1, 0:1])
                    e.reg_mul(r2, r1, FF)
                    return e.dma_start(out=b1row[0:1, :],
                                       in_=AP(b1d, r2, [[FF, 1], [1, FF]]))
                P.step("gpsimd", b1gather, reads=[r_b2i], writes=[r_b1r],
                       dma=True)
                for ft in range(NF):
                    P.step("tensor",
                           lambda e, ft=ft:
                           e.matmul(ps_ut[:, ft:ft + 1],
                                    b1row[0:1, ft * 128:(ft + 1) * 128],
                                    ones_c[0:1, 0:1], start=True, stop=True,
                                    skip_group_check=True),
                           reads=[r_b1r, r_consts],
                           writes=[r_put] if ft == 0 else (),
                           inc=(ft == NF - 1))
                _merge(r_put.ready, tM, tM.n)
                P.step("vector",
                       lambda e: e.tensor_copy(b1cols, ps_ut[:, 0:NF]),
                       reads=[r_put], writes=[r_b1c])
                for sp in range(NSTRIP):
                    for fpo in range(NF):
                        sl = fpo % 2
                        for dt in range(ND):
                            P.step("tensor",
                                   lambda e, ps=ps_sc[sl], fpo=fpo, dt=dt, sp=sp:
                                   e.matmul(ps[:, 0:STRIP],
                                            w1b3[:, dt,
                                                 fpo * 128:(fpo + 1) * 128],
                                            x2T3[:, dt,
                                                 sp * STRIP:(sp + 1) * STRIP],
                                            start=(dt == 0), stop=(dt == ND - 1)),
                                   reads=[r_w1b, r_x2T],
                                   writes=[r_ps[sl]] if dt == 0 else (),
                                   inc=(dt == ND - 1))
                        _merge(r_ps[sl].ready, tM, tM.n)
                        P.step("scalar",
                               lambda e, ps=ps_sc[sl], fpo=fpo:
                               e.activation(hT3[:, fpo, :], ps[:, 0:STRIP],
                                            AF.Gelu, bias=b1cols[:, fpo:fpo + 1]),
                               reads=[r_ps[sl], r_b1c],
                               writes=[r_hT] if fpo == 0 else ())
                    _merge(r_hT.ready, sA, sA.n)
                    for dpo in range(ND):
                        wc, rwc = w2c3[dpo % 2], r_w2c[dpo % 2]
                        P.dma("sync", wc,
                              AP(scr_w2[k], dpo * 128,
                                 [[D, 128], [128 * D, NF], [1, 128]]),
                              reads=[r_scrw2[k]], writes=[rwc])
                        yoff = 512 * (dpo % 2)
                        for ft in range(NF):
                            P.step("tensor",
                                   lambda e, yoff=yoff, wc=wc, ft=ft:
                                   e.matmul(ps_big[:, yoff:yoff + STRIP],
                                            wc[:, ft, :], hT3[:, ft, :],
                                            start=(ft == 0), stop=(ft == NF - 1)),
                                   reads=[rwc, r_hT],
                                   writes=[r_big] if ft == 0 else (),
                                   inc=(ft == NF - 1))
                        _merge(r_big.ready, tM, tM.n)
                        if k == 0:
                            P.step("vector",
                                   lambda e, yoff=yoff, dpo=dpo:
                                   e.tensor_scalar(yev[:, 0:STRIP],
                                                   ps_big[:, yoff:yoff + STRIP],
                                                   gcol[:, 0:1],
                                                   b2cols[:, dpo:dpo + 1],
                                                   ALU.mult, ALU.add),
                                   reads=[r_big, r_b2c, r_gcol],
                                   writes=[r_yev])
                            _merge(r_yev.ready, vE, vE.n)
                            P.dma("sync",
                                  AP(scr_ffn, (dpo * 128) * L + sp * STRIP,
                                     [[L, 128], [1, STRIP]]),
                                  yev, reads=[r_yev], writes=[r_scrf])
                        else:
                            P.step("vector",
                                   lambda e, yoff=yoff, dpo=dpo:
                                   e.tensor_scalar(yev[:, 0:STRIP],
                                                   ps_big[:, yoff:yoff + STRIP],
                                                   gcol[:, 1:2], None,
                                                   ALU.mult),
                                   reads=[r_big, r_gcol],
                                   writes=[r_yev])
                            _merge(r_yev.ready, vE, vE.n)
                            P.dma("gpsimd",
                                  AP(scr_ffn, (dpo * 128) * L + sp * STRIP,
                                     [[L, 128], [1, STRIP]]),
                                  yev, reads=[r_yev, r_scrf],
                                  writes=[r_scrf], accum_op=ALU.add)

            for kb in range(NT if cut >= 6 else 0):
                P.dma("sync", yT3,
                      AP(scr_ffn, kb * 128, [[L, 128], [128 * L, ND], [1, 128]]),
                      reads=[r_scrf], writes=[r_yT])
                P.dma("sync", x_f[0],
                      scr_x2.ap()[b, kb * 128:(kb + 1) * 128, :],
                      reads=[r_scrx2], writes=[r_xf[0]])
                for lo in range(0, ND, 4):
                    n = min(4, ND - lo)
                    for j in range(n):
                        P.step("tensor",
                               lambda e, dt=lo + j, j=j:
                               e.transpose(ps_tf[:, j * 128:(j + 1) * 128],
                                           yT_f[:, dt * 128:(dt + 1) * 128],
                                           id_f),
                               reads=[r_yT, r_consts],
                               writes=[r_ptf] if j == 0 else (),
                               inc=(j == n - 1))
                    _merge(r_ptf.ready, tM, tM.n)
                    P.step("vector",
                           lambda e, lo=lo, n=n:
                           e.tensor_copy(fn_f[:, lo * 128:(lo + n) * 128],
                                         ps_tf[:, 0:n * 128]),
                           reads=[r_ptf],
                           writes=[r_fn] if lo == 0 else ())
                P.step("vector", lambda e: e.tensor_add(t2_f, fn_f, x_f[0]),
                       reads=[r_fn, r_xf[0]], writes=[r_t2])
                P.step("scalar",
                       lambda e: e.activation(sq_f, t2_f, AF.Square,
                                              accum_out=stat[:, 1:2]),
                       reads=[r_t2, r_stat], writes=[r_sq, r_stat])
                P.step("vector",
                       lambda e: e.reduce_sum(stat[:, 0:1], t2_f, axis=AX.X),
                       reads=[r_t2, r_stat], writes=[r_stat])
                P.step("vector",
                       lambda e: e.tensor_scalar(stat[:, 2:3], stat[:, 0:1],
                                                 1.0 / D, None, ALU.mult),
                       reads=[r_stat], writes=[r_stat])
                P.step("vector",
                       lambda e: e.scalar_tensor_tensor(stat[:, 4:5],
                                                        stat[:, 2:3], 1.0,
                                                        stat[:, 2:3],
                                                        ALU.mult, ALU.mult),
                       reads=[r_stat], writes=[r_stat])
                P.step("vector",
                       lambda e: e.scalar_tensor_tensor(stat[:, 5:6],
                                                        stat[:, 1:2], 1.0 / D,
                                                        stat[:, 4:5],
                                                        ALU.mult, ALU.subtract),
                       reads=[r_stat], writes=[r_stat])
                P.step("scalar",
                       lambda e: e.activation(stat[:, 6:7], stat[:, 5:6],
                                              AF.Sqrt, bias=eps_sb),
                       reads=[r_stat, r_consts], writes=[r_stat])
                P.step("vector",
                       lambda e: e.reciprocal(stat[:, 7:8], stat[:, 6:7]),
                       reads=[r_stat], writes=[r_stat])
                P.step("vector",
                       lambda e: e.tensor_scalar(o_f, t2_f, stat[:, 2:3],
                                                 stat[:, 7:8], ALU.subtract,
                                                 ALU.mult),
                       reads=[r_t2, r_stat], writes=[r_of])
                P.dma("sync", yout.ap()[b, kb * 128:(kb + 1) * 128, :], o_f,
                      reads=[r_of], writes=[r_y])
            P.fence()

        if cut < 6:
            dummy_out()
        allsems = [(s, s.n) for s in P.dma_sems]
        P.step("sync", lambda e: e.nop(), extra=allsems, inc=False)
        P.step("gpsimd", lambda e: e.nop(), extra=allsems, inc=False)

        stats = dict(nops=P.nops, per_engine={e: len(P.items[e])
                                              for e in ENGINES},
                     pers=PERS_END, attn=ATTN_END, ffn=FFN_END)
        P.replay()
    nc.finalize()
    return nc, stats


def host_prep(cfg, x, Wq, Wk, Wv, Wr, u_bias, v_bias, fWr, fW1, fb1, fW2, fb2):
    B = cfg["BPC"]
    L, D, H, FF, E = (cfg[k] for k in ("L", "D", "H", "FF", "E"))
    Lr = 2 * L - 1
    NG = H // 2
    NB = x.shape[0]
    ncores = NB // B
    inv_freq = 1.0 / (10000.0 ** (np.arange(0, D, 2, dtype=np.float32) /
                                  np.float32(D)))
    pos = np.arange(-(L - 1), L, dtype=np.float32)
    ang = pos[:, None] * inv_freq[None, :]
    R = np.zeros((Lr, D), dtype=np.float32)
    R[:, 0::2] = np.sin(ang)
    R[:, 1::2] = np.cos(ang)
    common = {
        "wqT": np.ascontiguousarray(np.asarray(Wq).T, np.float32),
        "wkT": np.ascontiguousarray(np.asarray(Wk).T, np.float32),
        "wvT": np.ascontiguousarray(np.asarray(Wv).T, np.float32),
        "wrT": np.ascontiguousarray(np.asarray(Wr).T, np.float32),
        "rT": np.ascontiguousarray(R.T, np.float32),
        "ucols": np.ascontiguousarray(
            np.asarray(u_bias, np.float32).reshape(-1).reshape(NG, 128).T),
        "vcols": np.ascontiguousarray(
            np.asarray(v_bias, np.float32).reshape(-1).reshape(NG, 128).T),
        "wrffn": np.ascontiguousarray(np.asarray(fWr), np.float32),
        "w1": np.ascontiguousarray(np.asarray(fW1).reshape(E * D, FF),
                                   np.float32),
        "b1": np.ascontiguousarray(np.asarray(fb1), np.float32),
        "w2": np.ascontiguousarray(np.asarray(fW2).reshape(E * FF, D),
                                   np.float32),
        "b2": np.ascontiguousarray(np.asarray(fb2), np.float32),
    }
    xx = np.ascontiguousarray(np.asarray(x), np.float32)
    in_maps = []
    for c in range(ncores):
        m = dict(common)
        m["xin"] = np.ascontiguousarray(xx[c * B:(c + 1) * B])
        in_maps.append(m)
    return in_maps, ncores


def np_ref(cfg, x, Wq, Wk, Wv, Wr, u_bias, v_bias, fWr, fW1, fb1, fW2, fb2):
    from scipy.special import erf
    L, D, H, FF, E = (cfg[k] for k in ("L", "D", "H", "FF", "E"))
    DH = 64
    Lr = 2 * L - 1
    NB = x.shape[0]
    scale = 1.0 / np.sqrt(np.float32(DH))
    inv_freq = 1.0 / (10000.0 ** (np.arange(0, D, 2, dtype=np.float32) /
                                  np.float32(D)))
    pos = np.arange(-(L - 1), L, dtype=np.float32)
    ang = pos[:, None] * inv_freq[None, :]
    R = np.zeros((Lr, D), dtype=np.float32)
    R[:, 0::2] = np.sin(ang)
    R[:, 1::2] = np.cos(ang)

    def relshift(z):
        Hh, Lq, Lrr = z.shape
        zp = np.concatenate([np.zeros((Hh, Lq, 1), z.dtype), z], axis=-1)
        return zp.reshape(Hh, Lrr + 1, Lq)[:, 1:].reshape(Hh, Lq, Lrr)[..., Lrr - Lq:]

    def layernorm(v):
        m = v.mean(-1, keepdims=True)
        var = v.var(-1, keepdims=True)
        return (v - m) / np.sqrt(var + 1e-5)

    out = np.zeros_like(x)
    rp = (R @ Wr.T).reshape(Lr, H, DH).transpose(1, 0, 2)
    for bi in range(NB):
        w = x[bi]
        q = (w @ Wq.T).reshape(L, H, DH).transpose(1, 0, 2)
        kk = (w @ Wk.T).reshape(L, H, DH).transpose(1, 0, 2)
        vv = (w @ Wv.T).reshape(L, H, DH).transpose(1, 0, 2)
        AC = np.einsum('hid,hjd->hij', q + u_bias[:, None, :], kk)
        BD = np.einsum('hid,hjd->hij', q + v_bias[:, None, :], rp)
        BD = relshift(BD)
        logits = (AC + BD) * scale
        p = np.exp(logits - logits.max(-1, keepdims=True))
        p = p / p.sum(-1, keepdims=True)
        o = np.einsum('hij,hjd->hid', p, vv).transpose(1, 0, 2).reshape(L, D)
        x2 = layernorm(w + o)
        pooled = x2.mean(axis=0)
        logitsr = pooled @ fWr
        order = np.argsort(-logitsr, kind="stable")
        e0, e1 = order[0], order[1]
        s0, s1 = logitsr[e0], logitsr[e1]
        ww = np.exp([s0, s1] - max(s0, s1))
        ww = ww / ww.sum()
        y = np.zeros((L, D), np.float32)
        for g, ei in ((ww[0], e0), (ww[1], e1)):
            h = x2 @ fW1[ei] + fb1[ei]
            h = 0.5 * h * (1 + erf(h / np.sqrt(2.0)))
            y += g * (h @ fW2[ei] + fb2[ei])
        out[bi] = layernorm(x2 + y)
    return out
_CACHE = {}


def kernel(L0_attn_w, L0_attn_Wq, L0_attn_Wk, L0_attn_Wv, L0_attn_Wr,
           L0_attn_u_bias, L0_attn_v_bias, L0_ffn_Wr, L0_ffn_W1, L0_ffn_b1,
           L0_ffn_W2, L0_ffn_b2):
    from concourse.bass_utils import run_bass_kernel_spmd
    cfg = dict(FULL)
    if "nc" not in _CACHE:
        _CACHE["nc"] = build(cfg)[0]
    nc = _CACHE["nc"]
    in_maps, ncores = host_prep(cfg, np.asarray(L0_attn_w), L0_attn_Wq,
                                L0_attn_Wk, L0_attn_Wv, L0_attn_Wr,
                                L0_attn_u_bias, L0_attn_v_bias, L0_ffn_Wr,
                                L0_ffn_W1, L0_ffn_b1, L0_ffn_W2, L0_ffn_b2)
    assert ncores == 8
    res = run_bass_kernel_spmd(nc, in_maps, core_ids=list(range(8)))
    out = np.concatenate([np.asarray(res.results[c]["yout"])
                          for c in range(8)], axis=0)
    return out.astype(np.float32, copy=False)


# version 07
# speedup vs baseline: 1.1218x; 1.1218x over previous
import contextlib
import numpy as np
import ml_dtypes
import concourse.bass as bass
import concourse.mybir as mybir
from concourse.ap import AP
from concourse.bass_core_rust import IndirectOffsetOnAxis

F32 = mybir.dt.float32
BF16 = mybir.dt.bfloat16
I32 = mybir.dt.int32
AF = mybir.ActivationFunctionType
ALU = mybir.AluOpType
AX = mybir.AxisListType
BF = ml_dtypes.bfloat16

ENGINES = ("sync", "tensor", "vector", "scalar", "gpsimd")

FULL = dict(BPC=2, L=2048, D=1024, H=16, FF=4096, E=8)
MINI = dict(BPC=2, L=256, D=256, H=4, FF=512, E=8)


class Sem:
    def __init__(self, stack, nc, name):
        self.h = stack.enter_context(nc.semaphore(name))
        self.n = 0


class Res:
    __slots__ = ("ready", "free", "name")

    def __init__(self, name=""):
        self.ready = {}
        self.free = {}
        self.name = name


def _merge(d, sem, cnt):
    if cnt > d.get(sem, -1):
        d[sem] = cnt


class Plan:
    def __init__(self, nc, sems, stack):
        self.nc = nc
        self.stack = stack
        self.items = {e: [] for e in ENGINES}
        self.sems = sems
        self.hw = {e: {} for e in ENGINES}
        self.dma_sems = set()
        self.res_sems = {}
        self.cur_dma_sem = {"sync": None, "gpsimd": None}
        self.nsem = 0
        self.nops = 0
        self.pending = {}
        self.base_extra = []

    def fence(self):
        sems = set(self.sems.values()) | self.dma_sems
        self.base_extra = [(s, s.n) for s in sems if s.n > 0]

    def res_sem(self, res, eng):
        key = (id(res), eng)
        s = self.res_sems.get(key)
        if s is None:
            s = Sem(self.stack, self.nc, f"dr{self.nsem}")
            self.nsem += 1
            self.res_sems[key] = s
            self.dma_sems.add(s)
        return s

    def step(self, eng, body, reads=(), writes=(), extra=(), inc=True,
             dma=False):
        if dma:
            if writes:
                sem = self.res_sem(writes[0], eng)
                self.cur_dma_sem[eng] = sem
            else:
                sem = self.cur_dma_sem[eng]
                assert sem is not None, "continuation DMA with no current sem"
            k = 16
        else:
            sem = self.sems[eng]
            k = 16 if eng in ("sync", "gpsimd") else 1
        waits = {}
        for r in reads:
            for s, c in r.ready.items():
                if c > waits.get(s, -1):
                    waits[s] = c
        for wb in writes:
            for d in (wb.free, wb.ready):
                for s, c in d.items():
                    if c > waits.get(s, -1):
                        waits[s] = c
        for s, c in list(extra) + self.base_extra:
            if c > waits.get(s, -1):
                waits[s] = c
        for s in list(waits):
            if s in self.dma_sems:
                waits[s] = s.n
        hww = self.hw[eng]
        wl = []
        for s, c in waits.items():
            if c > hww.get(s, -1):
                wl.append((s.h, c))
                hww[s] = c
        if inc:
            sem.n += k
            ih = sem.h
        else:
            ih = None

        def fn(e, body=body, wl=tuple(wl), ih=ih, k=k):
            for h, v in wl:
                e.wait_ge(h, v)
            ins = body(e)
            if ih is not None:
                ins.then_inc(ih, k)
        self.items[eng].append(fn)
        self.nops += 1
        if inc:
            for wb in writes:
                wb.ready = {sem: sem.n}
                wb.free = {}
            for r in reads:
                _merge(r.free, sem, sem.n)
            pr, pw = self.pending.get(eng, ((), ()))
            for r in pr:
                _merge(r.free, sem, sem.n)
            for wb in pw:
                _merge(wb.ready, sem, sem.n)
            self.pending[eng] = ([], [])
        else:
            pr, pw = self.pending.setdefault(eng, ([], []))
            pr.extend(reads)
            pw.extend(writes)
        return sem.n

    def dma(self, eng, out_ap, in_ap, reads=(), writes=(), extra=(), **kw):
        return self.step(
            eng, lambda e, o=out_ap, i=in_ap, kw=kw: e.dma_start(out=o, in_=i, **kw),
            reads=reads, writes=writes, extra=extra, dma=True)

    def replay(self):
        with self.nc.Block() as block:
            for eng in ENGINES:
                items = self.items[eng]

                def make(items):
                    def run(e):
                        for fn in items:
                            fn(e)
                    return run
                getattr(block, eng)(make(items))


def cdiv(a, b):
    return (a + b - 1) // b


def build(cfg, cut=6, acut=9):
    B, L, D, H, FF, E = (cfg[k] for k in ("BPC", "L", "D", "H", "FF", "E"))
    DH = 64
    Lr = 2 * L - 1
    SCALE = float(np.float32(1.0) / np.sqrt(np.float32(DH)))
    NT = L // 128
    NG = H // 2
    ND = D // 128
    NF = FF // 128
    STRIP = min(512, L)
    NSTRIP = L // STRIP
    QC = 2
    LCH = 512
    NJC = cdiv(L, LCH)
    NDC = cdiv(D, 512)

    nc = bass.Bass("TRN2", target_bir_lowering=False, debug=False)

    xin = nc.dram_tensor("xin", [B, L, D], F32, kind="ExternalInput")
    wqT = nc.dram_tensor("wqT", [D, D], F32, kind="ExternalInput")
    wkT = nc.dram_tensor("wkT", [D, D], F32, kind="ExternalInput")
    wvT = nc.dram_tensor("wvT", [D, D], F32, kind="ExternalInput")
    wrT = nc.dram_tensor("wrT", [D, D], F32, kind="ExternalInput")
    rT = nc.dram_tensor("rT", [D, Lr], F32, kind="ExternalInput")
    ucols_t = nc.dram_tensor("ucols", [128, NG], F32, kind="ExternalInput")
    vcols_t = nc.dram_tensor("vcols", [128, NG], F32, kind="ExternalInput")
    wrffn = nc.dram_tensor("wrffn", [D, E], F32, kind="ExternalInput")
    w1d = nc.dram_tensor("w1", [E * D, FF], F32, kind="ExternalInput")
    b1d = nc.dram_tensor("b1", [E, FF], F32, kind="ExternalInput")
    w2d = nc.dram_tensor("w2", [E * FF, D], F32, kind="ExternalInput")
    b2d = nc.dram_tensor("b2", [E, D], F32, kind="ExternalInput")
    yout = nc.dram_tensor("yout", [B, L, D], F32, kind="ExternalOutput")

    idbf_t = nc.inline_tensor(np.eye(128, dtype=BF), "idbf")
    idf_t = nc.inline_tensor(np.eye(128, dtype=np.float32), "idf32")
    onesr_t = nc.inline_tensor(np.ones((1, 128), np.float32), "onesr")
    onesc_t = nc.inline_tensor(np.ones((128, 1), np.float32), "onesc")
    iota8_t = nc.inline_tensor(np.arange(E, dtype=np.float32).reshape(1, E), "iota8")
    iotac_t = nc.inline_tensor(np.arange(128, dtype=np.float32).reshape(128, 1),
                               "iotac")

    scr_bnc = [nc.dram_tensor(f"scr_bnc{i}", [L * 2 * L], BF16, kind="Internal")
               for i in range(2)]
    scr_rp = nc.dram_tensor("scr_rp", [NG, 128, Lr], BF16, kind="Internal")
    scr_attn = nc.dram_tensor("scr_attn", [B * L * D], BF16, kind="Internal")
    scr_x2 = nc.dram_tensor("scr_x2", [B, L, D], F32, kind="Internal")
    scr_w2 = [nc.dram_tensor(f"scr_w2_{i}", [FF * D], BF16, kind="Internal")
              for i in range(2)]
    scr_ffn = nc.dram_tensor("scr_ffn", [D * L], F32, kind="Internal")

    st = contextlib.ExitStack()
    with st:
        dS = Sem(st, nc, "dS")
        dG = Sem(st, nc, "dG")
        tM = Sem(st, nc, "tM")
        vE = Sem(st, nc, "vE")
        sA = Sem(st, nc, "sA")
        P = Plan(nc, {"sync": dS, "gpsimd": dG, "tensor": tM, "vector": vE,
                      "scalar": sA}, st)

        _gregs = {}

        def greg(e, name):
            if name not in _gregs:
                _gregs[name] = e.alloc_register(name)
            return _gregs[name]

        ps_sc = [st.enter_context(nc.psum_tensor(f"psc{i}", [128, 512], F32))
                 for i in range(2)]
        ps_big = st.enter_context(nc.psum_tensor("ps_big", [128, 2048], F32))
        ps_tr = st.enter_context(nc.psum_tensor("ps_tr", [128, 1024], BF16))
        ps_av = st.enter_context(nc.psum_tensor("ps_av", [128, 512], F32))
        ps_ut = ps_av
        ps_tf = ps_big
        r_ps = [Res(f"ps{i}") for i in range(2)]
        r_big = Res("ps_big")
        r_ptr = Res("ps_tr")
        r_pav = Res("ps_av")
        r_put = Res("ps_ut")
        r_ptf = r_big

        SB_TOTAL = 194 * 1024
        SB_BASE = nc.bump_sbuf(SB_TOTAL)[0]
        assert SB_BASE + SB_TOTAL <= nc.SBUF_PARTITION_SIZE_BYTES - 4096, SB_BASE

        def rnd(x):
            return (x + 31) & ~31

        def salloc(name, cols, dtype, off):
            esz = {F32: 4, BF16: 2, I32: 4}[dtype]
            assert off + cols * esz <= SB_TOTAL, \
                f"{name}: {off + cols * esz} > {SB_TOTAL}"
            return nc.alloc_sbuf_tensor_at(name, [128, cols], dtype,
                                           offset=SB_BASE + off).ap()

        class Reg:
            def __init__(self, base):
                self.o = [base]

            def al(self, name, cols, dtype):
                esz = {F32: 4, BF16: 2, I32: 4}[dtype]
                ap = salloc(name, cols, dtype, self.o[0])
                self.o[0] = rnd(self.o[0] + cols * esz)
                return ap

        PR = Reg(0)
        u_sb = PR.al("u_sb", NG, F32)
        v_sb = PR.al("v_sb", NG, F32)
        eps_sb = PR.al("eps_sb", 1, F32)
        ebias = PR.al("ebias", 1, F32)
        id_bf = PR.al("id_bf", 128, BF16)
        id_f = PR.al("id_f", 128, F32)
        ones_r = PR.al("ones_r", 128, F32)
        ones_c = PR.al("ones_c", 1, F32)
        iota_r = PR.al("iota_r", E, F32)
        iota_c = PR.al("iota_c", 1, F32)
        ssl = [PR.al(f"ssl{i}", NT, F32) for i in range(2)]
        rsl = [PR.al(f"rsl{i}", NT, F32) for i in range(2)]
        gcol = PR.al("gcol", 2, F32)
        ecol = PR.al("ecol", 2, F32)
        b1cols = PR.al("b1cols", NF, F32)
        b2cols = PR.al("b2cols", ND, F32)
        idx1 = [PR.al(f"idx1_{k}", ND, I32) for k in range(2)]
        idx2 = [PR.al(f"idx2_{k}", NF, I32) for k in range(2)]
        tmpc = PR.al("tmpc", 8, F32)
        b2i = PR.al("b2i", 2, I32)
        x2T = PR.al("x2T", ND * L, BF16)
        PERS_END = PR.o[0]

        AR = Reg(rnd(PERS_END))
        xT = AR.al("xT", ND * L, BF16)
        quT = AR.al("quT", L, BF16)
        qvT = AR.al("qvT", L, BF16)
        kTb = AR.al("kTb", L, BF16)
        vTp = AR.al("vTp", L, BF16)
        vsl = AR.al("vsl", NT * 128, BF16)
        rp_pair = [AR.al(f"rp_p{i}", Lr, BF16) for i in range(2)]
        w_sl = AR.al("w_sl", 2 * ND * 128, BF16)
        rawb = [[AR.al(f"raw{h}_{i}", 2304, BF16) for i in range(2)]
                for h in range(2)]
        bdsh = [[AR.al(f"bdsh{h}_{i}", L, BF16) for i in range(2)]
                for h in range(2)]
        ptile = [AR.al(f"pt{h}", L, BF16) for h in range(2)]
        ptT = [AR.al(f"ptT{h}", QC * NT * 128, BF16) for h in range(2)]
        otp = AR.al("otp", L, F32)
        otp2 = AR.al("otp2", L, F32)
        att_sb = AR.al("att_sb", 128, BF16)
        xb_sb = [AR.al(f"xb{i}", D, BF16) for i in range(2)]
        ATTN_END = AR.o[0]

        FR = Reg(rnd(PERS_END))
        w1b = FR.al("w1b", ND * FF, BF16)
        hT_off = FR.o[0]
        hT = FR.al("hT", NF * STRIP, BF16)
        HTOFF = rnd(PERS_END)
        w2c = [FR.al(f"w2c{i}", NF * 128, BF16) for i in range(2)]
        x_f = [FR.al(f"x_f{i}", D, F32) for i in range(2)]
        at_f = [FR.al(f"at_f{i}", D, BF16) for i in range(2)]
        t_f = FR.al("t_f", D, F32)
        x2_f = FR.al("x2_f", D, F32)
        x2b_f = FR.al("x2b_f", D, BF16)
        stat = FR.al("stat", 16, F32)
        pool_r = FR.al("pool_r", D, F32)
        wrf_sb = FR.al("wrf_sb", E, F32)
        rout = FR.al("rout", 64, F32)
        yev = FR.al("yev", STRIP, F32)
        yT_f = FR.al("yT_f", D, F32)
        fn_f = FR.al("fn_f", D, F32)
        t2_f = FR.al("t2_f", D, F32)
        o_f = FR.al("o_f", D, F32)
        need = FF * 4 + 2 * D * 4 + D * 2 + D * 4
        if NF * STRIP * 2 >= need:
            b1row = salloc("b1row", FF, F32, hT_off)
            b2row = [salloc(f"b2row{i}", D, F32, hT_off + FF * 4 + i * D * 4)
                     for i in range(2)]
            w2stage = salloc("w2stage", D, BF16, hT_off + FF * 4 + 2 * D * 4)
            sq_f = salloc("sq_f", D, F32,
                          hT_off + FF * 4 + 2 * D * 4 + D * 2)
        else:
            b1row = FR.al("b1row", FF, F32)
            b2row = [FR.al(f"b2row{i}", D, F32) for i in range(2)]
            w2stage = FR.al("w2stage", D, BF16)
            sq_f = FR.al("sq_f", D, F32)
        FFN_END = FR.o[0]
        assert ATTN_END <= SB_TOTAL and FFN_END <= SB_TOTAL, \
            (PERS_END, ATTN_END, FFN_END, SB_TOTAL)

        r_consts = Res("consts")
        r_x2T = Res("x2T")
        r_ssl = [Res("ssl0"), Res("ssl1")]
        r_rsl = [Res("rsl0"), Res("rsl1")]
        r_gcol = Res("gcol")
        r_ecol = Res("ecol")
        r_b1c = Res("b1c")
        r_b2c = Res("b2c")
        r_idx1 = [Res("idx1a"), Res("idx1b")]
        r_idx2 = [Res("idx2a"), Res("idx2b")]
        r_tmpc = Res("tmpc")
        r_b2i = Res("b2i")
        r_xT = Res("xT")
        r_qu = Res("qu")
        r_qv = Res("qv")
        r_kT = Res("kT")
        r_vTp = Res("vTp")
        r_vsl = Res("vsl")
        r_rpp = [Res("rp0"), Res("rp1")]
        r_wsl = [Res("w_sl0"), Res("w_sl1")]
        r_raw = [[Res(f"raw{h}{i}") for i in range(2)] for h in range(2)]
        r_bdsh = [[Res(f"bdsh{h}{i}") for i in range(2)] for h in range(2)]
        r_pt = [Res("pt0"), Res("pt1")]
        r_ptT = [Res("ptT0"), Res("ptT1")]
        r_otp = Res("otp")
        r_att = Res("att")
        r_xb = [Res("xb0"), Res("xb1")]
        r_ring = [Res("ring0"), Res("ring1")]
        r_scr_rp = Res("scr_rp")
        r_scra = Res("scra")
        r_scrx2 = Res("scrx2")
        r_scrw2 = [Res("scrw2a"), Res("scrw2b")]
        r_scrf = Res("scrf")
        r_xf = [Res("xf0"), Res("xf1")]
        r_atf = [Res("atf0"), Res("atf1")]
        r_tf = Res("tf")
        r_x2f = Res("x2f")
        r_x2b = Res("x2b")
        r_stat = Res("stat")
        r_pool = Res("pool")
        r_wrf = Res("wrf")
        r_rout = Res("rout")
        r_w1b = Res("w1b")
        r_hT = Res("hT")
        r_w2c = [Res("w2c0"), Res("w2c1")]
        r_yev = Res("yev")
        r_yT = Res("yT")
        r_fn = Res("fn")
        r_t2 = Res("t2")
        r_of = Res("of")
        r_sq = r_hT
        r_b1r = r_hT
        r_b2r = [r_hT, r_hT]
        r_w2s = r_hT
        r_y = Res("yout")

        xT3 = xT.rearrange("p (d l) -> p d l", d=ND)
        x2T3 = x2T.rearrange("p (d l) -> p d l", d=ND)
        w1b3 = w1b.rearrange("p (d f) -> p d f", d=ND)
        hT3 = hT.rearrange("p (f s) -> p f s", f=NF)
        w2c3 = [w2c[i].rearrange("p (f c) -> p f c", f=NF) for i in range(2)]
        ptT4 = [ptT[h].rearrange("p (a j i) -> p a j i", a=QC, j=NT)
                for h in range(2)]
        yT3 = yT_f.rearrange("p (d c) -> p d c", d=ND)

        for dst, t in ((u_sb, ucols_t), (v_sb, vcols_t), (id_bf, idbf_t),
                       (id_f, idf_t), (ones_c, onesc_t), (iota_c, iotac_t)):
            P.dma("sync", dst, t.ap()[:, :], writes=[r_consts])
        P.dma("sync", ones_r[0:1, :], onesr_t.ap()[:, :], writes=[r_consts])
        P.dma("sync", iota_r[0:1, :], iota8_t.ap()[:, :], writes=[r_consts])
        P.step("vector", lambda e: e.memset(eps_sb, 1e-5), reads=[r_consts])
        P.step("vector", lambda e: e.memset(ebias, -12.0), reads=[r_consts])
        _merge(r_consts.ready, vE, vE.n)
        CE = list(r_consts.ready.items())

        RP = Reg(rnd(PERS_END))
        rtb = [RP.al(f"rtb{dt}", Lr, BF16) for dt in range(ND)]
        wr_sl = RP.al("wr_sl", ND * 128, BF16)
        rp_out = RP.al("rp_out", Lr, BF16)
        assert RP.o[0] <= SB_TOTAL
        r_rtb = [Res(f"rtb{dt}") for dt in range(ND)]
        r_wr = Res("wr")
        r_rpo = Res("rpo")

        for dt in range(ND):
            P.dma("gpsimd", rtb[dt], rT.ap()[dt * 128:(dt + 1) * 128, :],
                  writes=[r_rtb[dt]], extra=CE)
        NRC = cdiv(Lr, 512)
        for g in range(NG):
            for dt in range(ND):
                P.dma("gpsimd", wr_sl[:, dt * 128:(dt + 1) * 128],
                      wrT.ap()[dt * 128:(dt + 1) * 128, g * 128:(g + 1) * 128],
                      writes=[r_wr])
            for rc in range(NRC):
                w = min(512, Lr - rc * 512)
                sl = rc % 2
                for dt in range(ND):
                    P.step("tensor",
                           lambda e, ps=ps_sc[sl], dt=dt, rc=rc, w=w:
                           e.matmul(ps[:, 0:w], wr_sl[:, dt * 128:(dt + 1) * 128],
                                    rtb[dt][:, rc * 512:rc * 512 + w],
                                    start=(dt == 0), stop=(dt == ND - 1)),
                           reads=[r_wr, r_rtb[dt]],
                           writes=[r_ps[sl]] if dt == 0 else (),
                           inc=(dt == ND - 1))
                _merge(r_ps[sl].ready, tM, tM.n)
                P.step("vector",
                       lambda e, ps=ps_sc[sl], rc=rc, w=w:
                       e.tensor_copy(rp_out[:, rc * 512:rc * 512 + w], ps[:, 0:w]),
                       reads=[r_ps[sl]], writes=[r_rpo] if rc == 0 else ())
            _merge(r_rpo.ready, vE, vE.n)
            P.dma("sync", scr_rp.ap()[g, :, :], rp_out, reads=[r_rpo],
                  writes=[r_scr_rp])
        P.fence()

        def dummy_out():
            P.step("vector", lambda e: e.memset(o_f, 0.0), writes=[r_of])
            for bb in range(B):
                for kb in range(NT):
                    P.dma("sync", yout.ap()[bb, kb * 128:(kb + 1) * 128, :],
                          o_f, reads=[r_of], writes=[r_y])

        for b in range(B if cut >= 2 else 0):
            for stp in range(NT):
                xb, rxb = xb_sb[stp % 2], r_xb[stp % 2]
                P.dma("gpsimd", xb, xin.ap()[b, stp * 128:(stp + 1) * 128, :],
                      writes=[rxb], extra=CE)
                for lo in range(0, ND, 8):
                    n = min(8, ND - lo)
                    for j in range(n):
                        P.step("tensor",
                               lambda e, xb=xb, dt=lo + j, j=j:
                               e.transpose(ps_tr[:, j * 128:(j + 1) * 128],
                                           xb[:, dt * 128:(dt + 1) * 128], id_bf),
                               reads=[rxb, r_consts],
                               writes=[r_ptr] if j == 0 else (),
                               inc=(j == n - 1))
                    _merge(r_ptr.ready, tM, tM.n)
                    P.step("vector",
                           lambda e, lo=lo, n=n, stp=stp:
                           e.tensor_copy(xT3[:, lo:lo + n,
                                             stp * 128:(stp + 1) * 128],
                                         ps_tr[:, 0:n * 128]),
                           reads=[r_ptr],
                           writes=[r_xT] if (stp == 0 and lo == 0) else ())
            _merge(r_xT.ready, vE, vE.n)

            for g in range(NG if cut >= 3 else 0):
                hseq = b * NG + g
                ring = scr_bnc[hseq % 2]
                rring = r_ring[hseq % 2]
                rp_b, r_rp = rp_pair[g % 2], r_rpp[g % 2]
                P.dma("sync", rp_b, scr_rp.ap()[g, :, :],
                      reads=[r_scr_rp], writes=[r_rp])

                def proj(wdram, outs, rress, biases, wi):
                    wsl_b = w_sl[:, wi * ND * 128:(wi + 1) * ND * 128]
                    r_w = r_wsl[wi]
                    for dt in range(ND):
                        P.dma("gpsimd", wsl_b[:, dt * 128:(dt + 1) * 128],
                              wdram.ap()[dt * 128:(dt + 1) * 128,
                                         g * 128:(g + 1) * 128],
                              writes=[r_w])
                    for fc in range(cdiv(L, 512)):
                        w = min(512, L - fc * 512)
                        sl = fc % 4
                        poff = 512 * sl
                        for dt in range(ND):
                            P.step("tensor",
                                   lambda e, poff=poff, dt=dt, fc=fc, w=w,
                                   wsl_b=wsl_b:
                                   e.matmul(ps_big[:, poff:poff + w],
                                            wsl_b[:, dt * 128:(dt + 1) * 128],
                                            xT3[:, dt, fc * 512:fc * 512 + w],
                                            start=(dt == 0), stop=(dt == ND - 1)),
                                   reads=[r_w, r_xT],
                                   writes=[r_big] if dt == 0 else (),
                                   inc=(dt == ND - 1))
                        _merge(r_big.ready, tM, tM.n)
                        for ob, rr, bias in zip(outs, rress, biases):
                            if bias is None:
                                P.step("scalar",
                                       lambda e, poff=poff, fc=fc, ob=ob, w=w:
                                       e.activation(ob[:, fc * 512:fc * 512 + w],
                                                    ps_big[:, poff:poff + w],
                                                    AF.Copy),
                                       reads=[r_big],
                                       writes=[rr] if fc == 0 else ())
                            else:
                                P.step("vector",
                                       lambda e, poff=poff, fc=fc, ob=ob,
                                       bias=bias, w=w:
                                       e.tensor_scalar(
                                           ob[:, fc * 512:fc * 512 + w],
                                           ps_big[:, poff:poff + w],
                                           bias, None, ALU.add),
                                       reads=[r_big, r_consts],
                                       writes=[rr] if fc == 0 else ())
                    for rr in rress:
                        _merge(rr.ready, sA, sA.n)
                        _merge(rr.ready, vE, vE.n)

                proj(wqT, [quT, qvT], [r_qu, r_qv],
                     [u_sb[:, g:g + 1], v_sb[:, g:g + 1]], 0)
                proj(wkT, [kTb], [r_kT], [None], 1)
                proj(wvT, [vTp], [r_vTp], [None], 0)
                for lo in range(0, NT, 8):
                    n = min(8, NT - lo)
                    for j in range(n):
                        P.step("tensor",
                               lambda e, jt=lo + j, j=j:
                               e.transpose(ps_tr[:, j * 128:(j + 1) * 128],
                                           vTp[:, jt * 128:(jt + 1) * 128], id_bf),
                               reads=[r_vTp, r_consts],
                               writes=[r_ptr] if j == 0 else (),
                               inc=(j == n - 1))
                    _merge(r_ptr.ready, tM, tM.n)
                    P.step("vector",
                           lambda e, lo=lo, n=n:
                           e.tensor_copy(vsl[:, lo * 128:(lo + n) * 128],
                                         ps_tr[:, 0:n * 128]),
                           reads=[r_ptr],
                           writes=[r_vsl] if lo == 0 else ())
                _merge(r_vsl.ready, vE, vE.n)

                def windows(t):
                    i0 = t * 128
                    hw_ = L - 1 - i0
                    tw_ = i0 + 128
                    wins = []
                    c = 0
                    while c < hw_:
                        w = min(512, hw_ - c)
                        wins.append((1 + c, c, w))
                        c += w
                    c = 0
                    while c < tw_:
                        w = min(512, tw_ - c)
                        wins.append((1 + hw_ + c, Lr - tw_ + c, w))
                        c += w
                    return wins, hw_, tw_

                def bd_memset(t):
                    for hh in range(2):
                        P.step("vector",
                               lambda e, rb=rawb[hh][t % 2]:
                               e.memset(rb[:, 0:1], 0.0),
                               writes=[r_raw[hh][t % 2]])

                def bd_chunk(t, win):
                    doff, rpc, w = win
                    rbs = [rawb[0][t % 2], rawb[1][t % 2]]
                    rrs = [r_raw[0][t % 2], r_raw[1][t % 2]]
                    P.step("tensor",
                           lambda e, rpc=rpc, w=w, t=t, rp=rp_b:
                           e.matmul(ps_sc[0][:, 0:w],
                                    qvT[0:64, t * 128:(t + 1) * 128],
                                    rp[0:64, rpc:rpc + w],
                                    start=True, stop=True,
                                    tile_position=(0, 0)),
                           reads=[r_qv, r_rp],
                           writes=[r_ps[0]], inc=False)
                    P.step("tensor",
                           lambda e, rpc=rpc, w=w, t=t, rp=rp_b:
                           e.matmul(ps_sc[1][:, 0:w],
                                    qvT[64:128, t * 128:(t + 1) * 128],
                                    rp[64:128, rpc:rpc + w],
                                    start=True, stop=True,
                                    tile_position=(64, 0)),
                           reads=[r_qv, r_rp],
                           writes=[r_ps[1]], inc=True)
                    _merge(r_ps[0].ready, tM, tM.n)
                    _merge(r_ps[1].ready, tM, tM.n)
                    for hh in range(2):
                        P.step("vector",
                               lambda e, ps=ps_sc[hh], rb=rbs[hh], doff=doff,
                               w=w:
                               e.tensor_copy(rb[:, doff:doff + w], ps[:, 0:w]),
                               reads=[r_ps[hh]], writes=[rrs[hh]])
                        _merge(r_ps[hh].free, vE, vE.n)

                def bd_writes(t):
                    i0 = t * 128
                    _, hw_, tw_ = windows(t)
                    for hh in range(2):
                        rb, rr = rawb[hh][t % 2], r_raw[hh][t % 2]
                        P.dma("sync",
                              AP(ring, i0 * 2 * L, [[2 * L, 128], [1, hw_ + 1]]),
                              rb[:, 0:hw_ + 1], reads=[rr], writes=[rring])
                        P.dma("sync",
                              AP(ring, i0 * 2 * L + (2 * L - tw_),
                                 [[2 * L, 128], [1, tw_]]),
                              rb[:, hw_ + 1:hw_ + 1 + tw_], reads=[rr],
                              writes=[rring])

                def sc_reads(t):
                    i0 = t * 128
                    for hh in range(2):
                        src = AP(ring, i0 * (2 * L - 1) + (2 * L - 1),
                                 [[2 * L - 1, 128], [1, L]])
                        P.dma("sync", bdsh[hh][t % 2], src, reads=[rring],
                              writes=[r_bdsh[hh][t % 2]])

                def sc_chunk(t, jc, emit_tr=True):
                    w = min(LCH, L - jc * LCH)
                    co = 512 * (jc % 2)
                    P.step("tensor",
                           lambda e, co=co, jc=jc, w=w, t=t:
                           e.matmul(ps_big[:, co:co + w],
                                    quT[0:64, t * 128:(t + 1) * 128],
                                    kTb[0:64, jc * LCH:jc * LCH + w],
                                    start=True, stop=False,
                                    tile_position=(0, 0)),
                           reads=[r_qu, r_kT],
                           writes=[r_big], inc=False)
                    P.step("tensor",
                           lambda e, co=co, jc=jc, w=w, t=t:
                           e.matmul(ps_big[:, 1024 + co:1024 + co + w],
                                    quT[64:128, t * 128:(t + 1) * 128],
                                    kTb[64:128, jc * LCH:jc * LCH + w],
                                    start=True, stop=False,
                                    tile_position=(64, 0)),
                           reads=[r_qu, r_kT],
                           writes=[r_big], inc=False)
                    for hh in range(2):
                        P.step("tensor",
                               lambda e, co=co, hh=hh, jc=jc, w=w, t=t:
                               e.matmul(ps_big[:, hh * 1024 + co:
                                               hh * 1024 + co + w], id_bf,
                                        bdsh[hh][t % 2][:,
                                                        jc * LCH:jc * LCH + w],
                                        start=False, stop=True),
                               reads=[r_bdsh[hh][t % 2], r_consts],
                               inc=(hh == 1))
                    _merge(r_big.ready, tM, tM.n)
                    if jc % 2 == 1 or jc == NJC - 1:
                        p0 = (jc // 2) * 2
                        pw = sum(min(LCH, L - c * LCH)
                                 for c in range(p0, jc + 1))
                        for hh in range(2):
                            acc = (ssl[hh][:, t:t + 1] if p0 == 0
                                   else tmpc[:, 4 + hh:5 + hh])
                            racc = r_ssl[hh] if p0 == 0 else r_tmpc
                            P.step("scalar",
                                   lambda e, hh=hh, p0=p0, pw=pw, t=t, acc=acc:
                                   e.activation(
                                       ptile[hh][:, p0 * LCH:p0 * LCH + pw],
                                       ps_big[:, hh * 1024:hh * 1024 + pw],
                                       AF.Exp, bias=ebias, scale=SCALE,
                                       accum_out=acc),
                                   reads=[r_big, r_consts] +
                                         ([r_ssl[hh]] if p0 == 0 else [r_tmpc]),
                                   writes=[r_pt[hh], racc])
                            if p0 > 0:
                                P.step("vector",
                                       lambda e, hh=hh, t=t:
                                       e.tensor_add(ssl[hh][:, t:t + 1],
                                                    ssl[hh][:, t:t + 1],
                                                    tmpc[:, 4 + hh:5 + hh]),
                                       reads=[r_tmpc, r_ssl[hh]],
                                       writes=[r_ssl[hh]])
                        _merge(r_big.free, sA, sA.n)

                def sc_tr(t, jc):
                    if acut < 4:
                        return
                    w = min(LCH, L - jc * LCH)
                    njt = (w + 127) // 128
                    for hh in range(2):
                        for jj in range(njt):
                            P.step("tensor",
                                   lambda e, hh=hh, jt=jc * 4 + jj, jj=jj:
                                   e.transpose(
                                       ps_tr[:, (hh * 4 + jj) * 128:
                                             (hh * 4 + jj + 1) * 128],
                                       ptile[hh][:, jt * 128:(jt + 1) * 128],
                                       id_bf),
                                   reads=[r_pt[hh], r_consts],
                                   writes=[r_ptr] if jj == 0 else (),
                                   inc=(jj == njt - 1))
                        _merge(r_ptr.ready, tM, tM.n)
                        P.step("vector",
                               lambda e, hh=hh, jc=jc, njt=njt, tq=t % QC:
                               e.tensor_copy(
                                   ptT4[hh][:, tq, jc * 4:jc * 4 + njt, :],
                                   ps_tr[:, hh * 512:hh * 512 + njt * 128]),
                               reads=[r_ptr],
                               writes=[r_ptT[hh]]
                               if (t % QC == 0 and jc == 0) else ())
                        _merge(r_ptT[hh].ready, vE, vE.n)
                        _merge(r_pt[hh].free, vE, vE.n)
                        _merge(r_ptr.free, vE, vE.n)

                def av_emit(t):
                    if acut < 5:
                        return
                    q = t // QC
                    for hh in range(2):
                        _merge(r_ptT[hh].ready, vE, vE.n)
                    for hh, dstb in ((0, otp), (1, otp2)):
                        for jt in range(NT):
                            P.step("tensor",
                                   lambda e, hh=hh, jt=jt:
                                   e.matmul(ps_av[0:64, 0:QC * 128],
                                            vsl[:, jt * 128 + hh * 64:
                                                jt * 128 + hh * 64 + 64],
                                            ptT4[hh][:, :, jt, :],
                                            start=(jt == 0),
                                            stop=(jt == NT - 1)),
                                   reads=[r_ptT[hh], r_vsl],
                                   writes=[r_pav] if jt == 0 else (),
                                   inc=(jt == NT - 1))
                        _merge(r_pav.ready, tM, tM.n)
                        P.step("vector",
                               lambda e, q=q, dstb=dstb:
                               e.tensor_copy(dstb[0:64, q * QC * 128:
                                                  (q + 1) * QC * 128],
                                             ps_av[0:64, 0:QC * 128]),
                               reads=[r_pav])
                        _merge(r_pav.free, vE, vE.n)
                        _merge(r_ptT[hh].free, vE, vE.n)

                if acut >= 2:
                    for tt in (0, 1):
                        if tt < NT:
                            bd_memset(tt)
                            for win in windows(tt)[0]:
                                bd_chunk(tt, win)
                            bd_writes(tt)
                    if acut >= 3:
                        sc_reads(0)
                    for t in range(NT):
                        if acut < 3:
                            continue
                        tb = t + 2
                        bwins = windows(tb)[0] if tb < NT else []
                        if tb < NT:
                            bd_memset(tb)
                        nc_ = max(NJC, len(bwins))
                        for c in range(nc_):
                            if c < len(bwins):
                                bd_chunk(tb, bwins[c])
                            if c < NJC:
                                sc_chunk(t, c, emit_tr=False)
                            if 2 <= c < NJC + 2:
                                sc_tr(t, c - 2)
                        if tb < NT:
                            bd_writes(tb)
                        if t + 1 < NT:
                            sc_reads(t + 1)
                        for cc in range(max(0, min(nc_ - 2, NJC)), NJC):
                            sc_tr(t, cc)
                        if t % QC == QC - 1:
                            av_emit(t)
                _merge(r_otp.ready, vE, vE.n)

                if acut < 9:
                    continue
                for hh in range(2):
                    P.step("vector",
                           lambda e, hh=hh: e.reciprocal(rsl[hh], ssl[hh]),
                           reads=[r_ssl[hh]], writes=[r_rsl[hh]])
                for k in range(NT):
                    for hh in range(2):
                        P.step("tensor",
                               lambda e, hh=hh, k=k, src_=None:
                               e.transpose(ps_tf[:, hh * 64:hh * 64 + 64],
                                           (otp if hh == 0 else otp2)
                                           [0:64, k * 128:(k + 1) * 128],
                                           id_f[0:64, 0:64]),
                               reads=[r_otp, r_consts],
                               writes=[r_ptf] if hh == 0 else (),
                               inc=(hh == 1))
                    _merge(r_ptf.ready, tM, tM.n)
                    for hh in range(2):
                        P.step("vector",
                               lambda e, hh=hh, k=k:
                               e.tensor_scalar(att_sb[:, hh * 64:hh * 64 + 64],
                                               ps_tf[:, hh * 64:hh * 64 + 64],
                                               rsl[hh][:, k:k + 1], None,
                                               ALU.mult),
                               reads=[r_ptf, r_rsl[hh]],
                               writes=[r_att] if hh == 0 else ())
                    _merge(r_att.ready, vE, vE.n)
                    P.dma("sync",
                          AP(scr_attn, (b * L + k * 128) * D + g * 128,
                             [[D, 128], [1, 128]]),
                          att_sb, reads=[r_att], writes=[r_scra])

            P.fence()
            if cut < 4:
                continue

            for stp in range(NT):
                xf, rxf = x_f[stp % 2], r_xf[stp % 2]
                af, raf = at_f[stp % 2], r_atf[stp % 2]
                P.dma("sync", xf, xin.ap()[b, stp * 128:(stp + 1) * 128, :],
                      writes=[rxf], extra=CE)
                P.dma("sync", af,
                      AP(scr_attn, (b * L + stp * 128) * D, [[D, 128], [1, D]]),
                      reads=[r_scra], writes=[raf])
                P.step("vector", lambda e, xf=xf, af=af: e.tensor_add(t_f, xf, af),
                       reads=[rxf, raf], writes=[r_tf])
                P.step("scalar",
                       lambda e: e.activation(sq_f, t_f, AF.Square,
                                              accum_out=stat[:, 1:2]),
                       reads=[r_tf, r_stat], writes=[r_sq, r_stat])
                P.step("vector",
                       lambda e: e.reduce_sum(stat[:, 0:1], t_f, axis=AX.X),
                       reads=[r_tf, r_stat], writes=[r_stat])
                P.step("vector",
                       lambda e: e.tensor_scalar(stat[:, 2:3], stat[:, 0:1],
                                                 1.0 / D, None, ALU.mult),
                       reads=[r_stat], writes=[r_stat])
                P.step("vector",
                       lambda e: e.scalar_tensor_tensor(stat[:, 4:5],
                                                        stat[:, 2:3], 1.0,
                                                        stat[:, 2:3],
                                                        ALU.mult, ALU.mult),
                       reads=[r_stat], writes=[r_stat])
                P.step("vector",
                       lambda e: e.scalar_tensor_tensor(stat[:, 5:6],
                                                        stat[:, 1:2], 1.0 / D,
                                                        stat[:, 4:5],
                                                        ALU.mult, ALU.subtract),
                       reads=[r_stat], writes=[r_stat])
                P.step("scalar",
                       lambda e: e.activation(stat[:, 6:7], stat[:, 5:6],
                                              AF.Sqrt, bias=eps_sb),
                       reads=[r_stat, r_consts], writes=[r_stat])
                P.step("vector",
                       lambda e: e.reciprocal(stat[:, 7:8], stat[:, 6:7]),
                       reads=[r_stat], writes=[r_stat])
                P.step("vector",
                       lambda e: e.tensor_scalar(x2_f, t_f, stat[:, 2:3],
                                                 stat[:, 7:8], ALU.subtract,
                                                 ALU.mult),
                       reads=[r_tf, r_stat], writes=[r_x2f])
                P.dma("sync", scr_x2.ap()[b, stp * 128:(stp + 1) * 128, :], x2_f,
                      reads=[r_x2f], writes=[r_scrx2])
                for dc in range(NDC):
                    w = min(512, D - dc * 512)
                    pps = (ps_ut, ps_tf)[dc]
                    rps = (r_put, r_ptf)[dc]
                    P.step("tensor",
                           lambda e, pps=pps, dc=dc, w=w, stp=stp:
                           e.matmul(pps[0:1, 0:w], ones_c,
                                    x2_f[:, dc * 512:dc * 512 + w],
                                    start=(stp == 0), stop=(stp == NT - 1),
                                    skip_group_check=True),
                           reads=[r_x2f, r_consts],
                           writes=[rps] if stp == 0 else (),
                           inc=(dc == NDC - 1))
                    if stp == NT - 1:
                        _merge(rps.ready, tM, tM.n)
                _merge(r_x2f.free, tM, tM.n)
                P.step("vector", lambda e: e.tensor_copy(x2b_f, x2_f),
                       reads=[r_x2f], writes=[r_x2b])
                for lo in range(0, ND, 8):
                    n = min(8, ND - lo)
                    for j in range(n):
                        P.step("tensor",
                               lambda e, dt=lo + j, j=j:
                               e.transpose(ps_tr[:, j * 128:(j + 1) * 128],
                                           x2b_f[:, dt * 128:(dt + 1) * 128],
                                           id_bf),
                               reads=[r_x2b, r_consts],
                               writes=[r_ptr] if j == 0 else (),
                               inc=(j == n - 1))
                    _merge(r_ptr.ready, tM, tM.n)
                    P.step("vector",
                           lambda e, lo=lo, n=n, stp=stp:
                           e.tensor_copy(x2T3[:, lo:lo + n,
                                             stp * 128:(stp + 1) * 128],
                                         ps_tr[:, 0:n * 128]),
                           reads=[r_ptr],
                           writes=[r_x2T] if (stp == 0 and lo == 0) else ())
            _merge(r_x2T.ready, vE, vE.n)

            for dc in range(NDC):
                w = min(512, D - dc * 512)
                pps = (ps_ut, ps_tf)[dc]
                rps = (r_put, r_ptf)[dc]
                P.step("vector",
                       lambda e, pps=pps, dc=dc, w=w:
                       e.tensor_scalar(pool_r[0:1, dc * 512:dc * 512 + w],
                                       pps[0:1, 0:w], 1.0 / L, None, ALU.mult),
                       reads=[rps], writes=[r_pool] if dc == 0 else ())
            _merge(r_pool.ready, vE, vE.n)
            for dt in range(ND):
                P.step("tensor",
                       lambda e, dt=dt:
                       e.matmul(ps_ut[:, dt:dt + 1],
                                pool_r[0:1, dt * 128:(dt + 1) * 128],
                                ones_c[0:1, 0:1], start=True, stop=True,
                                skip_group_check=True),
                       reads=[r_pool, r_consts],
                       writes=[r_put] if dt == 0 else (),
                       inc=(dt == ND - 1))
            _merge(r_put.ready, tM, tM.n)
            P.step("vector",
                   lambda e: e.tensor_copy(pool_r[:, 0:ND], ps_ut[:, 0:ND]),
                   reads=[r_put], writes=[r_pool])
            for dt in range(ND):
                P.dma("sync", wrf_sb, wrffn.ap()[dt * 128:(dt + 1) * 128, :],
                      writes=[r_wrf], extra=CE)
                P.step("tensor",
                       lambda e, dt=dt:
                       e.matmul(ps_tf[0:1, 0:E], pool_r[:, dt:dt + 1],
                                wrf_sb, start=(dt == 0), stop=(dt == ND - 1),
                                skip_group_check=True),
                       reads=[r_wrf, r_pool],
                       writes=[r_ptf] if dt == 0 else (),
                       inc=True)
            _merge(r_ptf.ready, tM, tM.n)
            RO = r_rout
            P.step("vector",
                   lambda e: e.tensor_copy(rout[0:1, 0:E], ps_tf[0:1, 0:E]),
                   reads=[r_ptf], writes=[RO])
            P.step("vector",
                   lambda e: e.scalar_tensor_tensor(rout[0:1, 8:8 + E],
                                                    iota_r[0:1, :], -1e-7,
                                                    rout[0:1, 0:E],
                                                    ALU.mult, ALU.add),
                   reads=[RO, r_consts], writes=[RO])
            P.step("vector",
                   lambda e: e.reduce_max(rout[0:1, 16:17], rout[0:1, 8:8 + E],
                                          axis=AX.X),
                   reads=[RO], writes=[RO])
            P.step("vector",
                   lambda e: e.tensor_scalar(rout[0:1, 24:24 + E],
                                             rout[0:1, 8:8 + E],
                                             rout[0:1, 16:17], None, ALU.is_ge),
                   reads=[RO], writes=[RO])
            P.step("vector",
                   lambda e: e.scalar_tensor_tensor(rout[0:1, 32:32 + E],
                                                    rout[0:1, 24:24 + E], -1e30,
                                                    rout[0:1, 8:8 + E],
                                                    ALU.mult, ALU.add),
                   reads=[RO], writes=[RO])
            P.step("vector",
                   lambda e: e.reduce_max(rout[0:1, 17:18], rout[0:1, 32:32 + E],
                                          axis=AX.X),
                   reads=[RO], writes=[RO])
            P.step("vector",
                   lambda e: e.tensor_scalar(rout[0:1, 40:40 + E],
                                             rout[0:1, 32:32 + E],
                                             rout[0:1, 17:18], None, ALU.is_ge),
                   reads=[RO], writes=[RO])
            for dsti, ohoff in ((18, 24), (19, 40)):
                P.step("vector",
                       lambda e, dsti=dsti, ohoff=ohoff:
                       e.scalar_tensor_tensor(rout[0:1, 48:48 + E],
                                              rout[0:1, ohoff:ohoff + E], 1.0,
                                              rout[0:1, 0:E], ALU.mult, ALU.mult,
                                              accum_out=rout[0:1,
                                                             dsti:dsti + 1]),
                       reads=[RO], writes=[RO])
            for dsti, ohoff in ((20, 24), (21, 40)):
                P.step("vector",
                       lambda e, dsti=dsti, ohoff=ohoff:
                       e.scalar_tensor_tensor(rout[0:1, 48:48 + E],
                                              rout[0:1, ohoff:ohoff + E], 1.0,
                                              iota_r[0:1, :], ALU.mult, ALU.mult,
                                              accum_out=rout[0:1,
                                                             dsti:dsti + 1]),
                       reads=[RO, r_consts], writes=[RO])
            P.step("vector",
                   lambda e: e.tensor_sub(rout[0:1, 22:23], rout[0:1, 19:20],
                                          rout[0:1, 18:19]),
                   reads=[RO], writes=[RO])
            P.step("scalar",
                   lambda e: e.activation(rout[0:1, 23:24], rout[0:1, 22:23],
                                          AF.Sigmoid),
                   reads=[RO], writes=[RO])
            P.step("vector",
                   lambda e: e.tensor_scalar(rout[0:1, 26:27], rout[0:1, 23:24],
                                             -1.0, 1.0, ALU.mult, ALU.add),
                   reads=[RO], writes=[RO])
            for col, src in ((0, 26), (1, 23), (2, 20), (3, 21)):
                P.step("tensor",
                       lambda e, col=col, src=src:
                       e.matmul(ps_ut[:, col:col + 1], ones_r[0:1, :],
                                rout[0:1, src:src + 1], start=True, stop=True,
                                skip_group_check=True),
                       reads=[RO, r_consts],
                       writes=[r_put] if col == 0 else (),
                       inc=(col == 3))
            _merge(r_put.ready, tM, tM.n)
            P.step("vector", lambda e: e.tensor_copy(gcol, ps_ut[:, 0:2]),
                   reads=[r_put], writes=[r_gcol])
            P.step("vector", lambda e: e.tensor_copy(ecol, ps_ut[:, 2:4]),
                   reads=[r_put], writes=[r_ecol])
            P.step("vector",
                   lambda e: e.tensor_copy(b2i[0:1, 0:2], rout[0:1, 20:22]),
                   reads=[RO], writes=[r_b2i])
            for k in range(2):
                def b2gather(e, k=k):
                    r1 = greg(e, "gidx")
                    r2 = greg(e, "goff")
                    e.load(r1, b2i[0:1, k:k + 1])
                    e.reg_mul(r2, r1, D)
                    return e.dma_start(out=b2row[k][0:1, :],
                                       in_=AP(b2d, r2, [[D, 1], [1, D]]))
                P.step("gpsimd", b2gather, reads=[r_b2i], writes=[r_b2r[k]],
                       dma=True)
            P.step("vector",
                   lambda e: e.tensor_scalar(b2row[0][0:1, :], b2row[0][0:1, :],
                                             rout[0:1, 26:27], None, ALU.mult),
                   reads=[RO, r_b2r[0]], writes=[r_b2r[0]])
            P.step("vector",
                   lambda e: e.scalar_tensor_tensor(b2row[1][0:1, :],
                                                    b2row[1][0:1, :],
                                                    rout[0:1, 23:24],
                                                    b2row[0][0:1, :],
                                                    ALU.mult, ALU.add),
                   reads=[RO, r_b2r[0], r_b2r[1]], writes=[r_b2r[1]])
            for dt in range(ND):
                P.step("tensor",
                       lambda e, dt=dt:
                       e.matmul(ps_ut[:, 8 + dt:9 + dt],
                                b2row[1][0:1, dt * 128:(dt + 1) * 128],
                                ones_c[0:1, 0:1], start=True, stop=True,
                                skip_group_check=True),
                       reads=[r_b2r[1], r_consts],
                       writes=[r_put] if dt == 0 else (),
                       inc=(dt == ND - 1))
            _merge(r_put.ready, tM, tM.n)
            P.step("vector",
                   lambda e: e.tensor_copy(b2cols, ps_ut[:, 8:8 + ND]),
                   reads=[r_put], writes=[r_b2c])

            for k in range(2 if cut >= 5 else 0):
                def w1load(e, k=k):
                    r1 = greg(e, "gidx")
                    r2 = greg(e, "goff")
                    e.load(r1, b2i[0:1, k:k + 1])
                    e.reg_mul(r2, r1, D * FF)
                    src_ = AP(w1d, r2, [[FF, 128], [128 * FF, ND], [1, FF]])
                    return e.dma_start(out=w1b3[:, :, :], in_=src_)
                P.step("gpsimd", w1load, reads=[r_b2i],
                       writes=[r_w1b], dma=True)
                def w2cast(e, k=k):
                    r1 = greg(e, "gidx")
                    r2 = greg(e, "goff")
                    e.load(r1, b2i[0:1, k:k + 1])
                    e.reg_mul(r2, r1, FF * D)
                    return e.dma_start(out=AP(scr_w2[k], 0, [[1, FF * D]]),
                                       in_=AP(w2d, r2, [[1, FF * D]]))
                P.step("gpsimd", w2cast, reads=[r_b2i],
                       writes=[r_scrw2[k]], dma=True)
                P.step("vector",
                       lambda e, k=k: e.tensor_copy(b2i[0:1, 0:1],
                                                    rout[0:1, 20 + k:21 + k]),
                       reads=[RO], writes=[r_b2i])
                def b1gather(e):
                    r1 = greg(e, "gidx")
                    r2 = greg(e, "goff")
                    e.load(r1, b2i[0:1, 0:1])
                    e.reg_mul(r2, r1, FF)
                    return e.dma_start(out=b1row[0:1, :],
                                       in_=AP(b1d, r2, [[FF, 1], [1, FF]]))
                P.step("gpsimd", b1gather, reads=[r_b2i], writes=[r_b1r],
                       dma=True)
                for ft in range(NF):
                    P.step("tensor",
                           lambda e, ft=ft:
                           e.matmul(ps_ut[:, ft:ft + 1],
                                    b1row[0:1, ft * 128:(ft + 1) * 128],
                                    ones_c[0:1, 0:1], start=True, stop=True,
                                    skip_group_check=True),
                           reads=[r_b1r, r_consts],
                           writes=[r_put] if ft == 0 else (),
                           inc=(ft == NF - 1))
                _merge(r_put.ready, tM, tM.n)
                P.step("vector",
                       lambda e: e.tensor_copy(b1cols, ps_ut[:, 0:NF]),
                       reads=[r_put], writes=[r_b1c])
                for sp in range(NSTRIP):
                    for fpo in range(NF):
                        sl = fpo % 2
                        for dt in range(ND):
                            P.step("tensor",
                                   lambda e, ps=ps_sc[sl], fpo=fpo, dt=dt, sp=sp:
                                   e.matmul(ps[:, 0:STRIP],
                                            w1b3[:, dt,
                                                 fpo * 128:(fpo + 1) * 128],
                                            x2T3[:, dt,
                                                 sp * STRIP:(sp + 1) * STRIP],
                                            start=(dt == 0), stop=(dt == ND - 1)),
                                   reads=[r_w1b, r_x2T],
                                   writes=[r_ps[sl]] if dt == 0 else (),
                                   inc=(dt == ND - 1))
                        _merge(r_ps[sl].ready, tM, tM.n)
                        P.step("scalar",
                               lambda e, ps=ps_sc[sl], fpo=fpo:
                               e.activation(hT3[:, fpo, :], ps[:, 0:STRIP],
                                            AF.Gelu, bias=b1cols[:, fpo:fpo + 1]),
                               reads=[r_ps[sl], r_b1c],
                               writes=[r_hT] if fpo == 0 else ())
                    _merge(r_hT.ready, sA, sA.n)
                    for dpo in range(ND):
                        wc, rwc = w2c3[dpo % 2], r_w2c[dpo % 2]
                        P.dma("sync", wc,
                              AP(scr_w2[k], dpo * 128,
                                 [[D, 128], [128 * D, NF], [1, 128]]),
                              reads=[r_scrw2[k]], writes=[rwc])
                        yoff = 512 * (dpo % 2)
                        for ft in range(NF):
                            P.step("tensor",
                                   lambda e, yoff=yoff, wc=wc, ft=ft:
                                   e.matmul(ps_big[:, yoff:yoff + STRIP],
                                            wc[:, ft, :], hT3[:, ft, :],
                                            start=(ft == 0), stop=(ft == NF - 1)),
                                   reads=[rwc, r_hT],
                                   writes=[r_big] if ft == 0 else (),
                                   inc=(ft == NF - 1))
                        _merge(r_big.ready, tM, tM.n)
                        if k == 0:
                            P.step("vector",
                                   lambda e, yoff=yoff, dpo=dpo:
                                   e.tensor_scalar(yev[:, 0:STRIP],
                                                   ps_big[:, yoff:yoff + STRIP],
                                                   gcol[:, 0:1],
                                                   b2cols[:, dpo:dpo + 1],
                                                   ALU.mult, ALU.add),
                                   reads=[r_big, r_b2c, r_gcol],
                                   writes=[r_yev])
                            _merge(r_yev.ready, vE, vE.n)
                            P.dma("sync",
                                  AP(scr_ffn, (dpo * 128) * L + sp * STRIP,
                                     [[L, 128], [1, STRIP]]),
                                  yev, reads=[r_yev], writes=[r_scrf])
                        else:
                            P.step("vector",
                                   lambda e, yoff=yoff, dpo=dpo:
                                   e.tensor_scalar(yev[:, 0:STRIP],
                                                   ps_big[:, yoff:yoff + STRIP],
                                                   gcol[:, 1:2], None,
                                                   ALU.mult),
                                   reads=[r_big, r_gcol],
                                   writes=[r_yev])
                            _merge(r_yev.ready, vE, vE.n)
                            P.dma("gpsimd",
                                  AP(scr_ffn, (dpo * 128) * L + sp * STRIP,
                                     [[L, 128], [1, STRIP]]),
                                  yev, reads=[r_yev, r_scrf],
                                  writes=[r_scrf], accum_op=ALU.add)

            for kb in range(NT if cut >= 6 else 0):
                P.dma("sync", yT3,
                      AP(scr_ffn, kb * 128, [[L, 128], [128 * L, ND], [1, 128]]),
                      reads=[r_scrf], writes=[r_yT])
                P.dma("sync", x_f[0],
                      scr_x2.ap()[b, kb * 128:(kb + 1) * 128, :],
                      reads=[r_scrx2], writes=[r_xf[0]])
                for lo in range(0, ND, 4):
                    n = min(4, ND - lo)
                    for j in range(n):
                        P.step("tensor",
                               lambda e, dt=lo + j, j=j:
                               e.transpose(ps_tf[:, j * 128:(j + 1) * 128],
                                           yT_f[:, dt * 128:(dt + 1) * 128],
                                           id_f),
                               reads=[r_yT, r_consts],
                               writes=[r_ptf] if j == 0 else (),
                               inc=(j == n - 1))
                    _merge(r_ptf.ready, tM, tM.n)
                    P.step("vector",
                           lambda e, lo=lo, n=n:
                           e.tensor_copy(fn_f[:, lo * 128:(lo + n) * 128],
                                         ps_tf[:, 0:n * 128]),
                           reads=[r_ptf],
                           writes=[r_fn] if lo == 0 else ())
                P.step("vector", lambda e: e.tensor_add(t2_f, fn_f, x_f[0]),
                       reads=[r_fn, r_xf[0]], writes=[r_t2])
                P.step("scalar",
                       lambda e: e.activation(sq_f, t2_f, AF.Square,
                                              accum_out=stat[:, 1:2]),
                       reads=[r_t2, r_stat], writes=[r_sq, r_stat])
                P.step("vector",
                       lambda e: e.reduce_sum(stat[:, 0:1], t2_f, axis=AX.X),
                       reads=[r_t2, r_stat], writes=[r_stat])
                P.step("vector",
                       lambda e: e.tensor_scalar(stat[:, 2:3], stat[:, 0:1],
                                                 1.0 / D, None, ALU.mult),
                       reads=[r_stat], writes=[r_stat])
                P.step("vector",
                       lambda e: e.scalar_tensor_tensor(stat[:, 4:5],
                                                        stat[:, 2:3], 1.0,
                                                        stat[:, 2:3],
                                                        ALU.mult, ALU.mult),
                       reads=[r_stat], writes=[r_stat])
                P.step("vector",
                       lambda e: e.scalar_tensor_tensor(stat[:, 5:6],
                                                        stat[:, 1:2], 1.0 / D,
                                                        stat[:, 4:5],
                                                        ALU.mult, ALU.subtract),
                       reads=[r_stat], writes=[r_stat])
                P.step("scalar",
                       lambda e: e.activation(stat[:, 6:7], stat[:, 5:6],
                                              AF.Sqrt, bias=eps_sb),
                       reads=[r_stat, r_consts], writes=[r_stat])
                P.step("vector",
                       lambda e: e.reciprocal(stat[:, 7:8], stat[:, 6:7]),
                       reads=[r_stat], writes=[r_stat])
                P.step("vector",
                       lambda e: e.tensor_scalar(o_f, t2_f, stat[:, 2:3],
                                                 stat[:, 7:8], ALU.subtract,
                                                 ALU.mult),
                       reads=[r_t2, r_stat], writes=[r_of])
                P.dma("sync", yout.ap()[b, kb * 128:(kb + 1) * 128, :], o_f,
                      reads=[r_of], writes=[r_y])
            P.fence()

        if cut < 6:
            dummy_out()
        allsems = [(s, s.n) for s in P.dma_sems]
        P.step("sync", lambda e: e.nop(), extra=allsems, inc=False)
        P.step("gpsimd", lambda e: e.nop(), extra=allsems, inc=False)

        stats = dict(nops=P.nops, per_engine={e: len(P.items[e])
                                              for e in ENGINES},
                     pers=PERS_END, attn=ATTN_END, ffn=FFN_END)
        P.replay()
    nc.finalize()
    return nc, stats


def host_prep(cfg, x, Wq, Wk, Wv, Wr, u_bias, v_bias, fWr, fW1, fb1, fW2, fb2):
    B = cfg["BPC"]
    L, D, H, FF, E = (cfg[k] for k in ("L", "D", "H", "FF", "E"))
    Lr = 2 * L - 1
    NG = H // 2
    NB = x.shape[0]
    ncores = NB // B
    inv_freq = 1.0 / (10000.0 ** (np.arange(0, D, 2, dtype=np.float32) /
                                  np.float32(D)))
    pos = np.arange(-(L - 1), L, dtype=np.float32)
    ang = pos[:, None] * inv_freq[None, :]
    R = np.zeros((Lr, D), dtype=np.float32)
    R[:, 0::2] = np.sin(ang)
    R[:, 1::2] = np.cos(ang)
    common = {
        "wqT": np.ascontiguousarray(np.asarray(Wq).T, np.float32),
        "wkT": np.ascontiguousarray(np.asarray(Wk).T, np.float32),
        "wvT": np.ascontiguousarray(np.asarray(Wv).T, np.float32),
        "wrT": np.ascontiguousarray(np.asarray(Wr).T, np.float32),
        "rT": np.ascontiguousarray(R.T, np.float32),
        "ucols": np.ascontiguousarray(
            np.asarray(u_bias, np.float32).reshape(-1).reshape(NG, 128).T),
        "vcols": np.ascontiguousarray(
            np.asarray(v_bias, np.float32).reshape(-1).reshape(NG, 128).T),
        "wrffn": np.ascontiguousarray(np.asarray(fWr), np.float32),
        "w1": np.ascontiguousarray(np.asarray(fW1).reshape(E * D, FF),
                                   np.float32),
        "b1": np.ascontiguousarray(np.asarray(fb1), np.float32),
        "w2": np.ascontiguousarray(np.asarray(fW2).reshape(E * FF, D),
                                   np.float32),
        "b2": np.ascontiguousarray(np.asarray(fb2), np.float32),
    }
    xx = np.ascontiguousarray(np.asarray(x), np.float32)
    in_maps = []
    for c in range(ncores):
        m = dict(common)
        m["xin"] = np.ascontiguousarray(xx[c * B:(c + 1) * B])
        in_maps.append(m)
    return in_maps, ncores


def np_ref(cfg, x, Wq, Wk, Wv, Wr, u_bias, v_bias, fWr, fW1, fb1, fW2, fb2):
    from scipy.special import erf
    L, D, H, FF, E = (cfg[k] for k in ("L", "D", "H", "FF", "E"))
    DH = 64
    Lr = 2 * L - 1
    NB = x.shape[0]
    scale = 1.0 / np.sqrt(np.float32(DH))
    inv_freq = 1.0 / (10000.0 ** (np.arange(0, D, 2, dtype=np.float32) /
                                  np.float32(D)))
    pos = np.arange(-(L - 1), L, dtype=np.float32)
    ang = pos[:, None] * inv_freq[None, :]
    R = np.zeros((Lr, D), dtype=np.float32)
    R[:, 0::2] = np.sin(ang)
    R[:, 1::2] = np.cos(ang)

    def relshift(z):
        Hh, Lq, Lrr = z.shape
        zp = np.concatenate([np.zeros((Hh, Lq, 1), z.dtype), z], axis=-1)
        return zp.reshape(Hh, Lrr + 1, Lq)[:, 1:].reshape(Hh, Lq, Lrr)[..., Lrr - Lq:]

    def layernorm(v):
        m = v.mean(-1, keepdims=True)
        var = v.var(-1, keepdims=True)
        return (v - m) / np.sqrt(var + 1e-5)

    out = np.zeros_like(x)
    rp = (R @ Wr.T).reshape(Lr, H, DH).transpose(1, 0, 2)
    for bi in range(NB):
        w = x[bi]
        q = (w @ Wq.T).reshape(L, H, DH).transpose(1, 0, 2)
        kk = (w @ Wk.T).reshape(L, H, DH).transpose(1, 0, 2)
        vv = (w @ Wv.T).reshape(L, H, DH).transpose(1, 0, 2)
        AC = np.einsum('hid,hjd->hij', q + u_bias[:, None, :], kk)
        BD = np.einsum('hid,hjd->hij', q + v_bias[:, None, :], rp)
        BD = relshift(BD)
        logits = (AC + BD) * scale
        p = np.exp(logits - logits.max(-1, keepdims=True))
        p = p / p.sum(-1, keepdims=True)
        o = np.einsum('hij,hjd->hid', p, vv).transpose(1, 0, 2).reshape(L, D)
        x2 = layernorm(w + o)
        pooled = x2.mean(axis=0)
        logitsr = pooled @ fWr
        order = np.argsort(-logitsr, kind="stable")
        e0, e1 = order[0], order[1]
        s0, s1 = logitsr[e0], logitsr[e1]
        ww = np.exp([s0, s1] - max(s0, s1))
        ww = ww / ww.sum()
        y = np.zeros((L, D), np.float32)
        for g, ei in ((ww[0], e0), (ww[1], e1)):
            h = x2 @ fW1[ei] + fb1[ei]
            h = 0.5 * h * (1 + erf(h / np.sqrt(2.0)))
            y += g * (h @ fW2[ei] + fb2[ei])
        out[bi] = layernorm(x2 + y)
    return out


_CACHE = {}


def kernel(L0_attn_w, L0_attn_Wq, L0_attn_Wk, L0_attn_Wv, L0_attn_Wr,
           L0_attn_u_bias, L0_attn_v_bias, L0_ffn_Wr, L0_ffn_W1, L0_ffn_b1,
           L0_ffn_W2, L0_ffn_b2):
    from concourse.bass_utils import run_bass_kernel_spmd
    cfg = dict(FULL)
    if "nc" not in _CACHE:
        _CACHE["nc"] = build(cfg)[0]
    nc = _CACHE["nc"]
    in_maps, ncores = host_prep(cfg, np.asarray(L0_attn_w), L0_attn_Wq,
                                L0_attn_Wk, L0_attn_Wv, L0_attn_Wr,
                                L0_attn_u_bias, L0_attn_v_bias, L0_ffn_Wr,
                                L0_ffn_W1, L0_ffn_b1, L0_ffn_W2, L0_ffn_b2)
    assert ncores == 8
    res = run_bass_kernel_spmd(nc, in_maps, core_ids=list(range(8)))
    out = np.concatenate([np.asarray(res.results[c]["yout"])
                          for c in range(8)], axis=0)
    return out.astype(np.float32, copy=False)


# version 08
# speedup vs baseline: 1.2008x; 1.0705x over previous
import contextlib
import numpy as np
import ml_dtypes
import concourse.bass as bass
import concourse.mybir as mybir
from concourse.ap import AP
from concourse.bass_core_rust import IndirectOffsetOnAxis

F32 = mybir.dt.float32
BF16 = mybir.dt.bfloat16
I32 = mybir.dt.int32
AF = mybir.ActivationFunctionType
ALU = mybir.AluOpType
AX = mybir.AxisListType
BF = ml_dtypes.bfloat16

ENGINES = ("sync", "tensor", "vector", "scalar", "gpsimd")

FULL = dict(BPC=2, L=2048, D=1024, H=16, FF=4096, E=8)
MINI = dict(BPC=2, L=256, D=256, H=4, FF=512, E=8)


class Sem:
    def __init__(self, stack, nc, name):
        self.h = stack.enter_context(nc.semaphore(name))
        self.n = 0


class Res:
    __slots__ = ("ready", "free", "name")

    def __init__(self, name=""):
        self.ready = {}
        self.free = {}
        self.name = name


def _merge(d, sem, cnt):
    if cnt > d.get(sem, -1):
        d[sem] = cnt


class Plan:
    def __init__(self, nc, sems, stack):
        self.nc = nc
        self.stack = stack
        self.items = {e: [] for e in ENGINES}
        self.sems = sems
        self.hw = {e: {} for e in ENGINES}
        self.dma_sems = set()
        self.res_sems = {}
        self.cur_dma_sem = {"sync": None, "gpsimd": None}
        self.nsem = 0
        self.nops = 0
        self.pending = {}
        self.base_extra = []

    def fence(self):
        sems = set(self.sems.values()) | self.dma_sems
        self.base_extra = [(s, s.n) for s in sems if s.n > 0]

    def res_sem(self, res, eng):
        key = (id(res), eng)
        s = self.res_sems.get(key)
        if s is None:
            s = Sem(self.stack, self.nc, f"dr{self.nsem}")
            self.nsem += 1
            self.res_sems[key] = s
            self.dma_sems.add(s)
        return s

    def step(self, eng, body, reads=(), writes=(), extra=(), inc=True,
             dma=False):
        if dma:
            if writes:
                sem = self.res_sem(writes[0], eng)
                self.cur_dma_sem[eng] = sem
            else:
                sem = self.cur_dma_sem[eng]
                assert sem is not None, "continuation DMA with no current sem"
            k = 16
        else:
            sem = self.sems[eng]
            k = 16 if eng in ("sync", "gpsimd") else 1
        waits = {}
        for r in reads:
            for s, c in r.ready.items():
                if c > waits.get(s, -1):
                    waits[s] = c
        for wb in writes:
            for d in (wb.free, wb.ready):
                for s, c in d.items():
                    if c > waits.get(s, -1):
                        waits[s] = c
        for s, c in list(extra) + self.base_extra:
            if c > waits.get(s, -1):
                waits[s] = c
        for s in list(waits):
            if s in self.dma_sems:
                waits[s] = s.n
        hww = self.hw[eng]
        wl = []
        for s, c in waits.items():
            if c > hww.get(s, -1):
                wl.append((s.h, c))
                hww[s] = c
        if inc:
            sem.n += k
            ih = sem.h
        else:
            ih = None

        def fn(e, body=body, wl=tuple(wl), ih=ih, k=k):
            for h, v in wl:
                e.wait_ge(h, v)
            ins = body(e)
            if ih is not None:
                ins.then_inc(ih, k)
        self.items[eng].append(fn)
        self.nops += 1
        if inc:
            for wb in writes:
                wb.ready = {sem: sem.n}
                wb.free = {}
            for r in reads:
                _merge(r.free, sem, sem.n)
            pr, pw = self.pending.get(eng, ((), ()))
            for r in pr:
                _merge(r.free, sem, sem.n)
            for wb in pw:
                _merge(wb.ready, sem, sem.n)
            self.pending[eng] = ([], [])
        else:
            pr, pw = self.pending.setdefault(eng, ([], []))
            pr.extend(reads)
            pw.extend(writes)
        return sem.n

    def dma(self, eng, out_ap, in_ap, reads=(), writes=(), extra=(), **kw):
        return self.step(
            eng, lambda e, o=out_ap, i=in_ap, kw=kw: e.dma_start(out=o, in_=i, **kw),
            reads=reads, writes=writes, extra=extra, dma=True)

    def replay(self):
        with self.nc.Block() as block:
            for eng in ENGINES:
                items = self.items[eng]

                def make(items):
                    def run(e):
                        for fn in items:
                            fn(e)
                    return run
                getattr(block, eng)(make(items))


def cdiv(a, b):
    return (a + b - 1) // b


def build(cfg, cut=6, acut=9):
    B, L, D, H, FF, E = (cfg[k] for k in ("BPC", "L", "D", "H", "FF", "E"))
    DH = 64
    Lr = 2 * L - 1
    SCALE = float(np.float32(1.0) / np.sqrt(np.float32(DH)))
    NT = L // 128
    NG = H // 2
    ND = D // 128
    NF = FF // 128
    STRIP = min(512, L)
    NSTRIP = L // STRIP
    QC = 2
    LCH = 512
    NJC = cdiv(L, LCH)
    NDC = cdiv(D, 512)

    nc = bass.Bass("TRN2", target_bir_lowering=False, debug=False)

    xin = nc.dram_tensor("xin", [B, L, D], F32, kind="ExternalInput")
    wqT = nc.dram_tensor("wqT", [D, D], F32, kind="ExternalInput")
    wkT = nc.dram_tensor("wkT", [D, D], F32, kind="ExternalInput")
    wvT = nc.dram_tensor("wvT", [D, D], F32, kind="ExternalInput")
    wrT = nc.dram_tensor("wrT", [D, D], F32, kind="ExternalInput")
    rT = nc.dram_tensor("rT", [D, Lr], F32, kind="ExternalInput")
    ucols_t = nc.dram_tensor("ucols", [128, NG], F32, kind="ExternalInput")
    vcols_t = nc.dram_tensor("vcols", [128, NG], F32, kind="ExternalInput")
    wrffn = nc.dram_tensor("wrffn", [D, E], F32, kind="ExternalInput")
    w1d = nc.dram_tensor("w1", [E * D, FF], F32, kind="ExternalInput")
    b1d = nc.dram_tensor("b1", [E, FF], F32, kind="ExternalInput")
    w2d = nc.dram_tensor("w2", [E * FF, D], F32, kind="ExternalInput")
    b2d = nc.dram_tensor("b2", [E, D], F32, kind="ExternalInput")
    yout = nc.dram_tensor("yout", [B, L, D], F32, kind="ExternalOutput")

    idbf_t = nc.inline_tensor(np.eye(128, dtype=BF), "idbf")
    idf_t = nc.inline_tensor(np.eye(128, dtype=np.float32), "idf32")
    onesr_t = nc.inline_tensor(np.ones((1, 128), np.float32), "onesr")
    onesc_t = nc.inline_tensor(np.ones((128, 1), np.float32), "onesc")
    iota8_t = nc.inline_tensor(np.arange(E, dtype=np.float32).reshape(1, E), "iota8")
    iotac_t = nc.inline_tensor(np.arange(128, dtype=np.float32).reshape(128, 1),
                               "iotac")

    scr_bnc = [nc.dram_tensor(f"scr_bnc{i}", [L * 2 * L], BF16, kind="Internal")
               for i in range(2)]
    scr_rp = nc.dram_tensor("scr_rp", [NG, 128, Lr], BF16, kind="Internal")
    scr_attn = nc.dram_tensor("scr_attn", [B * L * D], BF16, kind="Internal")
    scr_x2 = nc.dram_tensor("scr_x2", [B, L, D], F32, kind="Internal")
    scr_w2 = [nc.dram_tensor(f"scr_w2_{i}", [FF * D], BF16, kind="Internal")
              for i in range(2)]
    scr_ffn = nc.dram_tensor("scr_ffn", [D * L], F32, kind="Internal")

    st = contextlib.ExitStack()
    with st:
        dS = Sem(st, nc, "dS")
        dG = Sem(st, nc, "dG")
        tM = Sem(st, nc, "tM")
        vE = Sem(st, nc, "vE")
        sA = Sem(st, nc, "sA")
        P = Plan(nc, {"sync": dS, "gpsimd": dG, "tensor": tM, "vector": vE,
                      "scalar": sA}, st)

        _gregs = {}

        def greg(e, name):
            if name not in _gregs:
                _gregs[name] = e.alloc_register(name)
            return _gregs[name]

        ps_sc = [st.enter_context(nc.psum_tensor(f"psc{i}", [128, 512], F32))
                 for i in range(2)]
        ps_big = st.enter_context(nc.psum_tensor("ps_big", [128, 2048], F32))
        ps_tr = st.enter_context(nc.psum_tensor("ps_tr", [128, 1024], BF16))
        ps_av = st.enter_context(nc.psum_tensor("ps_av", [128, 512], F32))
        ps_ut = ps_av
        ps_tf = ps_big
        r_ps = [Res(f"ps{i}") for i in range(2)]
        r_big = Res("ps_big")
        r_ptr = Res("ps_tr")
        r_pav = Res("ps_av")
        r_put = Res("ps_ut")
        r_ptf = r_big

        SB_TOTAL = 194 * 1024
        SB_BASE = nc.bump_sbuf(SB_TOTAL)[0]
        assert SB_BASE + SB_TOTAL <= nc.SBUF_PARTITION_SIZE_BYTES - 4096, SB_BASE

        def rnd(x):
            return (x + 31) & ~31

        def salloc(name, cols, dtype, off):
            esz = {F32: 4, BF16: 2, I32: 4}[dtype]
            assert off + cols * esz <= SB_TOTAL, \
                f"{name}: {off + cols * esz} > {SB_TOTAL}"
            return nc.alloc_sbuf_tensor_at(name, [128, cols], dtype,
                                           offset=SB_BASE + off).ap()

        class Reg:
            def __init__(self, base):
                self.o = [base]

            def al(self, name, cols, dtype):
                esz = {F32: 4, BF16: 2, I32: 4}[dtype]
                ap = salloc(name, cols, dtype, self.o[0])
                self.o[0] = rnd(self.o[0] + cols * esz)
                return ap

        PR = Reg(0)
        u_sb = PR.al("u_sb", NG, F32)
        v_sb = PR.al("v_sb", NG, F32)
        eps_sb = PR.al("eps_sb", 1, F32)
        ebias = PR.al("ebias", 1, F32)
        id_bf = PR.al("id_bf", 128, BF16)
        id_f = PR.al("id_f", 128, F32)
        ones_r = PR.al("ones_r", 128, F32)
        ones_c = PR.al("ones_c", 1, F32)
        iota_r = PR.al("iota_r", E, F32)
        iota_c = PR.al("iota_c", 1, F32)
        ssl = [PR.al(f"ssl{i}", NT, F32) for i in range(2)]
        rsl = [PR.al(f"rsl{i}", NT, F32) for i in range(2)]
        gcol = PR.al("gcol", 2, F32)
        ecol = PR.al("ecol", 2, F32)
        b1cols = PR.al("b1cols", NF, F32)
        b2cols = PR.al("b2cols", ND, F32)
        idx1 = [PR.al(f"idx1_{k}", ND, I32) for k in range(2)]
        idx2 = [PR.al(f"idx2_{k}", NF, I32) for k in range(2)]
        tmpc = PR.al("tmpc", 8, F32)
        b2i = PR.al("b2i", 2, I32)
        x2T = PR.al("x2T", ND * L, BF16)
        PERS_END = PR.o[0]

        AR = Reg(rnd(PERS_END))
        xT = AR.al("xT", ND * L, BF16)
        quT = AR.al("quT", L, BF16)
        qvT = AR.al("qvT", L, BF16)
        kTb = AR.al("kTb", L, BF16)
        vTp = AR.al("vTp", L, BF16)
        vsl = AR.al("vsl", NT * 128, BF16)
        rp_pair = [AR.al(f"rp_p{i}", Lr, BF16) for i in range(2)]
        w_sl = AR.al("w_sl", 2 * ND * 128, BF16)
        rawb = [[AR.al(f"raw{h}_{i}", 2304, BF16) for i in range(2)]
                for h in range(2)]
        bdsh = [[AR.al(f"bdsh{h}_{i}", L, BF16) for i in range(2)]
                for h in range(2)]
        ptile = [AR.al(f"pt{h}", L, BF16) for h in range(2)]
        ptT = [AR.al(f"ptT{h}", QC * NT * 128, BF16) for h in range(2)]
        otp = AR.al("otp", L, F32)
        otp2 = AR.al("otp2", L, F32)
        att_sb = AR.al("att_sb", 128, BF16)
        xb_sb = [AR.al(f"xb{i}", D, BF16) for i in range(2)]
        ATTN_END = AR.o[0]

        FR = Reg(rnd(PERS_END))
        w1b = FR.al("w1b", ND * FF, BF16)
        hT_off = FR.o[0]
        hT = FR.al("hT", NF * STRIP, BF16)
        HTOFF = rnd(PERS_END)
        w2c = [FR.al(f"w2c{i}", NF * 128, BF16) for i in range(2)]
        x_f = [FR.al(f"x_f{i}", D, F32) for i in range(2)]
        at_f = [FR.al(f"at_f{i}", D, BF16) for i in range(2)]
        t_f = FR.al("t_f", D, F32)
        x2_f = FR.al("x2_f", D, F32)
        x2b_f = FR.al("x2b_f", D, BF16)
        stat = FR.al("stat", 16, F32)
        pool_r = FR.al("pool_r", D, F32)
        wrf_sb = FR.al("wrf_sb", E, F32)
        rout = FR.al("rout", 64, F32)
        yev = FR.al("yev", STRIP, F32)
        yT_f = FR.al("yT_f", D, F32)
        fn_f = FR.al("fn_f", D, F32)
        t2_f = FR.al("t2_f", D, F32)
        o_f = FR.al("o_f", D, F32)
        need = FF * 4 + 2 * D * 4 + D * 2 + D * 4
        if NF * STRIP * 2 >= need:
            b1row = salloc("b1row", FF, F32, hT_off)
            b2row = [salloc(f"b2row{i}", D, F32, hT_off + FF * 4 + i * D * 4)
                     for i in range(2)]
            w2stage = salloc("w2stage", D, BF16, hT_off + FF * 4 + 2 * D * 4)
            sq_f = salloc("sq_f", D, F32,
                          hT_off + FF * 4 + 2 * D * 4 + D * 2)
        else:
            b1row = FR.al("b1row", FF, F32)
            b2row = [FR.al(f"b2row{i}", D, F32) for i in range(2)]
            w2stage = FR.al("w2stage", D, BF16)
            sq_f = FR.al("sq_f", D, F32)
        FFN_END = FR.o[0]
        assert ATTN_END <= SB_TOTAL and FFN_END <= SB_TOTAL, \
            (PERS_END, ATTN_END, FFN_END, SB_TOTAL)

        r_consts = Res("consts")
        r_x2T = Res("x2T")
        r_ssl = [Res("ssl0"), Res("ssl1")]
        r_rsl = [Res("rsl0"), Res("rsl1")]
        r_gcol = Res("gcol")
        r_ecol = Res("ecol")
        r_b1c = Res("b1c")
        r_b2c = Res("b2c")
        r_idx1 = [Res("idx1a"), Res("idx1b")]
        r_idx2 = [Res("idx2a"), Res("idx2b")]
        r_tmpc = Res("tmpc")
        r_b2i = Res("b2i")
        r_xT = Res("xT")
        r_qu = Res("qu")
        r_qv = Res("qv")
        r_kT = Res("kT")
        r_vTp = Res("vTp")
        r_vsl = Res("vsl")
        r_rpp = [Res("rp0"), Res("rp1")]
        r_wsl = [Res("w_sl0"), Res("w_sl1")]
        r_raw = [[Res(f"raw{h}{i}") for i in range(2)] for h in range(2)]
        r_bdsh = [[Res(f"bdsh{h}{i}") for i in range(2)] for h in range(2)]
        r_pt = [Res("pt0"), Res("pt1")]
        r_ptT = [Res("ptT0"), Res("ptT1")]
        r_otp = Res("otp")
        r_att = Res("att")
        r_xb = [Res("xb0"), Res("xb1")]
        r_ring = [Res("ring0"), Res("ring1")]
        r_scr_rp = Res("scr_rp")
        r_scra = Res("scra")
        r_scrx2 = Res("scrx2")
        r_scrw2 = [Res("scrw2a"), Res("scrw2b")]
        r_scrf = Res("scrf")
        r_xf = [Res("xf0"), Res("xf1")]
        r_atf = [Res("atf0"), Res("atf1")]
        r_tf = Res("tf")
        r_x2f = Res("x2f")
        r_x2b = Res("x2b")
        r_stat = Res("stat")
        r_pool = Res("pool")
        r_wrf = Res("wrf")
        r_rout = Res("rout")
        r_w1b = Res("w1b")
        r_hT = Res("hT")
        r_w2c = [Res("w2c0"), Res("w2c1")]
        r_yev = Res("yev")
        r_yT = Res("yT")
        r_fn = Res("fn")
        r_t2 = Res("t2")
        r_of = Res("of")
        r_sq = r_hT
        r_b1r = r_hT
        r_b2r = [r_hT, r_hT]
        r_w2s = r_hT
        r_y = Res("yout")

        xT3 = xT.rearrange("p (d l) -> p d l", d=ND)
        x2T3 = x2T.rearrange("p (d l) -> p d l", d=ND)
        w1b3 = w1b.rearrange("p (d f) -> p d f", d=ND)
        hT3 = hT.rearrange("p (f s) -> p f s", f=NF)
        w2c3 = [w2c[i].rearrange("p (f c) -> p f c", f=NF) for i in range(2)]
        ptT4 = [ptT[h].rearrange("p (a j i) -> p a j i", a=QC, j=NT)
                for h in range(2)]
        yT3 = yT_f.rearrange("p (d c) -> p d c", d=ND)

        for dst, t in ((u_sb, ucols_t), (v_sb, vcols_t), (id_bf, idbf_t),
                       (id_f, idf_t), (ones_c, onesc_t), (iota_c, iotac_t)):
            P.dma("sync", dst, t.ap()[:, :], writes=[r_consts])
        P.dma("sync", ones_r[0:1, :], onesr_t.ap()[:, :], writes=[r_consts])
        P.dma("sync", iota_r[0:1, :], iota8_t.ap()[:, :], writes=[r_consts])
        P.step("vector", lambda e: e.memset(eps_sb, 1e-5), reads=[r_consts])
        P.step("vector", lambda e: e.memset(ebias, -12.0), reads=[r_consts])
        _merge(r_consts.ready, vE, vE.n)
        CE = list(r_consts.ready.items())

        RP = Reg(rnd(PERS_END))
        rtb = [RP.al(f"rtb{dt}", Lr, BF16) for dt in range(ND)]
        wr_sl = RP.al("wr_sl", ND * 128, BF16)
        rp_out = RP.al("rp_out", Lr, BF16)
        assert RP.o[0] <= SB_TOTAL
        r_rtb = [Res(f"rtb{dt}") for dt in range(ND)]
        r_wr = Res("wr")
        r_rpo = Res("rpo")

        for dt in range(ND):
            P.dma("gpsimd", rtb[dt], rT.ap()[dt * 128:(dt + 1) * 128, :],
                  writes=[r_rtb[dt]], extra=CE)
        NRC = cdiv(Lr, 512)
        for g in range(NG):
            for dt in range(ND):
                P.dma("gpsimd", wr_sl[:, dt * 128:(dt + 1) * 128],
                      wrT.ap()[dt * 128:(dt + 1) * 128, g * 128:(g + 1) * 128],
                      writes=[r_wr])
            for rc in range(NRC):
                w = min(512, Lr - rc * 512)
                sl = rc % 2
                for dt in range(ND):
                    P.step("tensor",
                           lambda e, ps=ps_sc[sl], dt=dt, rc=rc, w=w:
                           e.matmul(ps[:, 0:w], wr_sl[:, dt * 128:(dt + 1) * 128],
                                    rtb[dt][:, rc * 512:rc * 512 + w],
                                    start=(dt == 0), stop=(dt == ND - 1)),
                           reads=[r_wr, r_rtb[dt]],
                           writes=[r_ps[sl]] if dt == 0 else (),
                           inc=(dt == ND - 1))
                _merge(r_ps[sl].ready, tM, tM.n)
                P.step("vector",
                       lambda e, ps=ps_sc[sl], rc=rc, w=w:
                       e.tensor_copy(rp_out[:, rc * 512:rc * 512 + w], ps[:, 0:w]),
                       reads=[r_ps[sl]], writes=[r_rpo] if rc == 0 else ())
            _merge(r_rpo.ready, vE, vE.n)
            P.dma("sync", scr_rp.ap()[g, :, :], rp_out, reads=[r_rpo],
                  writes=[r_scr_rp])
        P.fence()

        def dummy_out():
            P.step("vector", lambda e: e.memset(o_f, 0.0), writes=[r_of])
            for bb in range(B):
                for kb in range(NT):
                    P.dma("sync", yout.ap()[bb, kb * 128:(kb + 1) * 128, :],
                          o_f, reads=[r_of], writes=[r_y])

        for b in range(B if cut >= 2 else 0):
            for stp in range(NT):
                xb, rxb = xb_sb[stp % 2], r_xb[stp % 2]
                P.dma("gpsimd", xb, xin.ap()[b, stp * 128:(stp + 1) * 128, :],
                      writes=[rxb], extra=CE)
                for lo in range(0, ND, 8):
                    n = min(8, ND - lo)
                    for j in range(n):
                        P.step("tensor",
                               lambda e, xb=xb, dt=lo + j, j=j:
                               e.transpose(ps_tr[:, j * 128:(j + 1) * 128],
                                           xb[:, dt * 128:(dt + 1) * 128], id_bf),
                               reads=[rxb, r_consts],
                               writes=[r_ptr] if j == 0 else (),
                               inc=(j == n - 1))
                    _merge(r_ptr.ready, tM, tM.n)
                    P.step("vector",
                           lambda e, lo=lo, n=n, stp=stp:
                           e.tensor_copy(xT3[:, lo:lo + n,
                                             stp * 128:(stp + 1) * 128],
                                         ps_tr[:, 0:n * 128]),
                           reads=[r_ptr],
                           writes=[r_xT] if (stp == 0 and lo == 0) else ())
            _merge(r_xT.ready, vE, vE.n)

            for g in range(NG if cut >= 3 else 0):
                hseq = b * NG + g
                ring = scr_bnc[hseq % 2]
                rring = r_ring[hseq % 2]
                rp_b, r_rp = rp_pair[g % 2], r_rpp[g % 2]
                P.dma("sync", rp_b, scr_rp.ap()[g, :, :],
                      reads=[r_scr_rp], writes=[r_rp])

                def proj(wdram, outs, rress, biases, wi):
                    wsl_b = w_sl[:, wi * ND * 128:(wi + 1) * ND * 128]
                    r_w = r_wsl[wi]
                    for dt in range(ND):
                        P.dma("gpsimd", wsl_b[:, dt * 128:(dt + 1) * 128],
                              wdram.ap()[dt * 128:(dt + 1) * 128,
                                         g * 128:(g + 1) * 128],
                              writes=[r_w])
                    for fc in range(cdiv(L, 512)):
                        w = min(512, L - fc * 512)
                        sl = fc % 2
                        psb = ps_sc[sl]
                        rps = r_ps[sl]
                        for dt in range(ND):
                            P.step("tensor",
                                   lambda e, psb=psb, dt=dt, fc=fc, w=w,
                                   wsl_b=wsl_b:
                                   e.matmul(psb[:, 0:w],
                                            wsl_b[:, dt * 128:(dt + 1) * 128],
                                            xT3[:, dt, fc * 512:fc * 512 + w],
                                            start=(dt == 0), stop=(dt == ND - 1)),
                                   reads=[r_w, r_xT],
                                   writes=[rps] if dt == 0 else (),
                                   inc=(dt == ND - 1))
                        _merge(rps.ready, tM, tM.n)
                        for ob, rr, bias in zip(outs, rress, biases):
                            if bias is None:
                                P.step("scalar",
                                       lambda e, psb=psb, fc=fc, ob=ob, w=w:
                                       e.activation(ob[:, fc * 512:fc * 512 + w],
                                                    psb[:, 0:w],
                                                    AF.Copy),
                                       reads=[rps],
                                       writes=[rr] if fc == 0 else ())
                            else:
                                P.step("vector",
                                       lambda e, psb=psb, fc=fc, ob=ob,
                                       bias=bias, w=w:
                                       e.tensor_scalar(
                                           ob[:, fc * 512:fc * 512 + w],
                                           psb[:, 0:w],
                                           bias, None, ALU.add),
                                       reads=[rps, r_consts],
                                       writes=[rr] if fc == 0 else ())
                        if biases[0] is None:
                            _merge(rps.free, sA, sA.n)
                        else:
                            _merge(rps.free, vE, vE.n)
                    for rr in rress:
                        _merge(rr.ready, sA, sA.n)
                        _merge(rr.ready, vE, vE.n)

                proj(wqT, [quT, qvT], [r_qu, r_qv],
                     [u_sb[:, g:g + 1], v_sb[:, g:g + 1]], 0)
                proj(wkT, [kTb], [r_kT], [None], 1)
                proj(wvT, [vTp], [r_vTp], [None], 0)
                for lo in range(0, NT, 8):
                    n = min(8, NT - lo)
                    for j in range(n):
                        P.step("tensor",
                               lambda e, jt=lo + j, j=j:
                               e.transpose(ps_tr[:, j * 128:(j + 1) * 128],
                                           vTp[:, jt * 128:(jt + 1) * 128], id_bf),
                               reads=[r_vTp, r_consts],
                               writes=[r_ptr] if j == 0 else (),
                               inc=(j == n - 1))
                    _merge(r_ptr.ready, tM, tM.n)
                    P.step("vector",
                           lambda e, lo=lo, n=n:
                           e.tensor_copy(vsl[:, lo * 128:(lo + n) * 128],
                                         ps_tr[:, 0:n * 128]),
                           reads=[r_ptr],
                           writes=[r_vsl] if lo == 0 else ())
                _merge(r_vsl.ready, vE, vE.n)

                def windows(t):
                    i0 = t * 128
                    hw_ = L - 1 - i0
                    tw_ = i0 + 128
                    wins = []
                    c = 0
                    while c < hw_:
                        w = min(512, hw_ - c)
                        wins.append((1 + c, c, w))
                        c += w
                    c = 0
                    while c < tw_:
                        w = min(512, tw_ - c)
                        wins.append((1 + hw_ + c, Lr - tw_ + c, w))
                        c += w
                    return wins, hw_, tw_

                def bd_memset(t):
                    for hh in range(2):
                        P.step("vector",
                               lambda e, rb=rawb[hh][t % 2]:
                               e.memset(rb[:, 0:1], 0.0),
                               writes=[r_raw[hh][t % 2]])

                def bd_chunk(t, win):
                    doff, rpc, w = win
                    rbs = [rawb[0][t % 2], rawb[1][t % 2]]
                    rrs = [r_raw[0][t % 2], r_raw[1][t % 2]]
                    P.step("tensor",
                           lambda e, rpc=rpc, w=w, t=t, rp=rp_b:
                           e.matmul(ps_sc[0][:, 0:w],
                                    qvT[0:64, t * 128:(t + 1) * 128],
                                    rp[0:64, rpc:rpc + w],
                                    start=True, stop=True,
                                    tile_position=(0, 0)),
                           reads=[r_qv, r_rp],
                           writes=[r_ps[0]], inc=False)
                    P.step("tensor",
                           lambda e, rpc=rpc, w=w, t=t, rp=rp_b:
                           e.matmul(ps_sc[1][:, 0:w],
                                    qvT[64:128, t * 128:(t + 1) * 128],
                                    rp[64:128, rpc:rpc + w],
                                    start=True, stop=True,
                                    tile_position=(64, 0)),
                           reads=[r_qv, r_rp],
                           writes=[r_ps[1]], inc=True)
                    _merge(r_ps[0].ready, tM, tM.n)
                    _merge(r_ps[1].ready, tM, tM.n)
                    for hh in range(2):
                        P.step("vector",
                               lambda e, ps=ps_sc[hh], rb=rbs[hh], doff=doff,
                               w=w:
                               e.tensor_copy(rb[:, doff:doff + w], ps[:, 0:w]),
                               reads=[r_ps[hh]], writes=[rrs[hh]])
                        _merge(r_ps[hh].free, vE, vE.n)

                def bd_writes(t):
                    i0 = t * 128
                    _, hw_, tw_ = windows(t)
                    for hh in range(2):
                        rb, rr = rawb[hh][t % 2], r_raw[hh][t % 2]
                        P.dma("sync",
                              AP(ring, i0 * 2 * L, [[2 * L, 128], [1, hw_ + 1]]),
                              rb[:, 0:hw_ + 1], reads=[rr], writes=[rring])
                        P.dma("sync",
                              AP(ring, i0 * 2 * L + (2 * L - tw_),
                                 [[2 * L, 128], [1, tw_]]),
                              rb[:, hw_ + 1:hw_ + 1 + tw_], reads=[rr],
                              writes=[rring])

                def sc_reads(t):
                    i0 = t * 128
                    for hh in range(2):
                        src = AP(ring, i0 * (2 * L - 1) + (2 * L - 1),
                                 [[2 * L - 1, 128], [1, L]])
                        P.dma("sync", bdsh[hh][t % 2], src, reads=[rring],
                              writes=[r_bdsh[hh][t % 2]])

                def sc_chunk(t, jc, emit_tr=True):
                    w = min(LCH, L - jc * LCH)
                    co = 512 * (jc % 2)
                    P.step("tensor",
                           lambda e, co=co, jc=jc, w=w, t=t:
                           e.matmul(ps_big[:, co:co + w],
                                    quT[0:64, t * 128:(t + 1) * 128],
                                    kTb[0:64, jc * LCH:jc * LCH + w],
                                    start=True, stop=False,
                                    tile_position=(0, 0)),
                           reads=[r_qu, r_kT],
                           writes=[r_big], inc=False)
                    P.step("tensor",
                           lambda e, co=co, jc=jc, w=w, t=t:
                           e.matmul(ps_big[:, 1024 + co:1024 + co + w],
                                    quT[64:128, t * 128:(t + 1) * 128],
                                    kTb[64:128, jc * LCH:jc * LCH + w],
                                    start=True, stop=False,
                                    tile_position=(64, 0)),
                           reads=[r_qu, r_kT],
                           writes=[r_big], inc=False)
                    for hh in range(2):
                        P.step("tensor",
                               lambda e, co=co, hh=hh, jc=jc, w=w, t=t:
                               e.matmul(ps_big[:, hh * 1024 + co:
                                               hh * 1024 + co + w], id_bf,
                                        bdsh[hh][t % 2][:,
                                                        jc * LCH:jc * LCH + w],
                                        start=False, stop=True),
                               reads=[r_bdsh[hh][t % 2], r_consts],
                               inc=(hh == 1))
                    _merge(r_big.ready, tM, tM.n)
                    if jc % 2 == 1 or jc == NJC - 1:
                        p0 = (jc // 2) * 2
                        pw = sum(min(LCH, L - c * LCH)
                                 for c in range(p0, jc + 1))
                        for hh in range(2):
                            acc = (ssl[hh][:, t:t + 1] if p0 == 0
                                   else tmpc[:, 4 + hh:5 + hh])
                            racc = r_ssl[hh] if p0 == 0 else r_tmpc
                            P.step("scalar",
                                   lambda e, hh=hh, p0=p0, pw=pw, t=t, acc=acc:
                                   e.activation(
                                       ptile[hh][:, p0 * LCH:p0 * LCH + pw],
                                       ps_big[:, hh * 1024:hh * 1024 + pw],
                                       AF.Exp, bias=ebias, scale=SCALE,
                                       accum_out=acc),
                                   reads=[r_big, r_consts] +
                                         ([r_ssl[hh]] if p0 == 0 else [r_tmpc]),
                                   writes=[r_pt[hh], racc])
                            if p0 > 0:
                                P.step("vector",
                                       lambda e, hh=hh, t=t:
                                       e.tensor_add(ssl[hh][:, t:t + 1],
                                                    ssl[hh][:, t:t + 1],
                                                    tmpc[:, 4 + hh:5 + hh]),
                                       reads=[r_tmpc, r_ssl[hh]],
                                       writes=[r_ssl[hh]])
                        _merge(r_big.free, sA, sA.n)

                def sc_tr(t, jc):
                    if acut < 4:
                        return
                    w = min(LCH, L - jc * LCH)
                    njt = (w + 127) // 128
                    for hh in range(2):
                        for jj in range(njt):
                            P.step("tensor",
                                   lambda e, hh=hh, jt=jc * 4 + jj, jj=jj:
                                   e.transpose(
                                       ps_tr[:, (hh * 4 + jj) * 128:
                                             (hh * 4 + jj + 1) * 128],
                                       ptile[hh][:, jt * 128:(jt + 1) * 128],
                                       id_bf),
                                   reads=[r_pt[hh], r_consts],
                                   writes=[r_ptr] if jj == 0 else (),
                                   inc=(jj == njt - 1))
                        _merge(r_ptr.ready, tM, tM.n)
                        P.step("vector",
                               lambda e, hh=hh, jc=jc, njt=njt, tq=t % QC:
                               e.tensor_copy(
                                   ptT4[hh][:, tq, jc * 4:jc * 4 + njt, :],
                                   ps_tr[:, hh * 512:hh * 512 + njt * 128]),
                               reads=[r_ptr],
                               writes=[r_ptT[hh]]
                               if (t % QC == 0 and jc == 0) else ())
                        _merge(r_ptT[hh].ready, vE, vE.n)
                        _merge(r_pt[hh].free, vE, vE.n)
                        _merge(r_ptr.free, vE, vE.n)

                def av_emit(t):
                    if acut < 5:
                        return
                    q = t // QC
                    for hh in range(2):
                        _merge(r_ptT[hh].ready, vE, vE.n)
                    for hh, dstb in ((0, otp), (1, otp2)):
                        for jt in range(NT):
                            P.step("tensor",
                                   lambda e, hh=hh, jt=jt:
                                   e.matmul(ps_av[0:64, 0:QC * 128],
                                            vsl[:, jt * 128 + hh * 64:
                                                jt * 128 + hh * 64 + 64],
                                            ptT4[hh][:, :, jt, :],
                                            start=(jt == 0),
                                            stop=(jt == NT - 1)),
                                   reads=[r_ptT[hh], r_vsl],
                                   writes=[r_pav] if jt == 0 else (),
                                   inc=(jt == NT - 1))
                        _merge(r_pav.ready, tM, tM.n)
                        P.step("vector",
                               lambda e, q=q, dstb=dstb:
                               e.tensor_copy(dstb[0:64, q * QC * 128:
                                                  (q + 1) * QC * 128],
                                             ps_av[0:64, 0:QC * 128]),
                               reads=[r_pav])
                        _merge(r_pav.free, vE, vE.n)
                        _merge(r_ptT[hh].free, vE, vE.n)

                if acut >= 2:
                    for tt in (0, 1):
                        if tt < NT:
                            bd_memset(tt)
                            for win in windows(tt)[0]:
                                bd_chunk(tt, win)
                            bd_writes(tt)
                    if acut >= 3:
                        sc_reads(0)
                    for t in range(NT):
                        if acut < 3:
                            continue
                        tb = t + 2
                        bwins = windows(tb)[0] if tb < NT else []
                        if tb < NT:
                            bd_memset(tb)
                        if t + 1 < NT:
                            sc_reads(t + 1)
                        nc_ = max(NJC, len(bwins))
                        for c in range(nc_):
                            if c < len(bwins):
                                bd_chunk(tb, bwins[c])
                            if c < NJC:
                                sc_chunk(t, c, emit_tr=False)
                            if 2 <= c < NJC + 2:
                                sc_tr(t, c - 2)
                        if tb < NT:
                            bd_writes(tb)
                        for cc in range(max(0, min(nc_ - 2, NJC)), NJC):
                            sc_tr(t, cc)
                        if t % QC == QC - 1:
                            av_emit(t)
                _merge(r_otp.ready, vE, vE.n)

                if acut < 9:
                    continue
                for hh in range(2):
                    P.step("vector",
                           lambda e, hh=hh: e.reciprocal(rsl[hh], ssl[hh]),
                           reads=[r_ssl[hh]], writes=[r_rsl[hh]])
                for k in range(NT):
                    for hh in range(2):
                        P.step("tensor",
                               lambda e, hh=hh, k=k, src_=None:
                               e.transpose(ps_tf[:, hh * 64:hh * 64 + 64],
                                           (otp if hh == 0 else otp2)
                                           [0:64, k * 128:(k + 1) * 128],
                                           id_f[0:64, 0:64]),
                               reads=[r_otp, r_consts],
                               writes=[r_ptf] if hh == 0 else (),
                               inc=(hh == 1))
                    _merge(r_ptf.ready, tM, tM.n)
                    for hh in range(2):
                        P.step("vector",
                               lambda e, hh=hh, k=k:
                               e.tensor_scalar(att_sb[:, hh * 64:hh * 64 + 64],
                                               ps_tf[:, hh * 64:hh * 64 + 64],
                                               rsl[hh][:, k:k + 1], None,
                                               ALU.mult),
                               reads=[r_ptf, r_rsl[hh]],
                               writes=[r_att] if hh == 0 else ())
                    _merge(r_att.ready, vE, vE.n)
                    P.dma("sync",
                          AP(scr_attn, (b * L + k * 128) * D + g * 128,
                             [[D, 128], [1, 128]]),
                          att_sb, reads=[r_att], writes=[r_scra])

            P.fence()
            if cut < 4:
                continue

            for stp in range(NT):
                xf, rxf = x_f[stp % 2], r_xf[stp % 2]
                af, raf = at_f[stp % 2], r_atf[stp % 2]
                P.dma("sync", xf, xin.ap()[b, stp * 128:(stp + 1) * 128, :],
                      writes=[rxf], extra=CE)
                P.dma("sync", af,
                      AP(scr_attn, (b * L + stp * 128) * D, [[D, 128], [1, D]]),
                      reads=[r_scra], writes=[raf])
                P.step("vector", lambda e, xf=xf, af=af: e.tensor_add(t_f, xf, af),
                       reads=[rxf, raf], writes=[r_tf])
                P.step("scalar",
                       lambda e: e.activation(sq_f, t_f, AF.Square,
                                              accum_out=stat[:, 1:2]),
                       reads=[r_tf, r_stat], writes=[r_sq, r_stat])
                P.step("vector",
                       lambda e: e.reduce_sum(stat[:, 0:1], t_f, axis=AX.X),
                       reads=[r_tf, r_stat], writes=[r_stat])
                P.step("vector",
                       lambda e: e.tensor_scalar(stat[:, 2:3], stat[:, 0:1],
                                                 1.0 / D, None, ALU.mult),
                       reads=[r_stat], writes=[r_stat])
                P.step("vector",
                       lambda e: e.scalar_tensor_tensor(stat[:, 4:5],
                                                        stat[:, 2:3], 1.0,
                                                        stat[:, 2:3],
                                                        ALU.mult, ALU.mult),
                       reads=[r_stat], writes=[r_stat])
                P.step("vector",
                       lambda e: e.scalar_tensor_tensor(stat[:, 5:6],
                                                        stat[:, 1:2], 1.0 / D,
                                                        stat[:, 4:5],
                                                        ALU.mult, ALU.subtract),
                       reads=[r_stat], writes=[r_stat])
                P.step("scalar",
                       lambda e: e.activation(stat[:, 6:7], stat[:, 5:6],
                                              AF.Sqrt, bias=eps_sb),
                       reads=[r_stat, r_consts], writes=[r_stat])
                P.step("vector",
                       lambda e: e.reciprocal(stat[:, 7:8], stat[:, 6:7]),
                       reads=[r_stat], writes=[r_stat])
                P.step("vector",
                       lambda e: e.tensor_scalar(x2_f, t_f, stat[:, 2:3],
                                                 stat[:, 7:8], ALU.subtract,
                                                 ALU.mult),
                       reads=[r_tf, r_stat], writes=[r_x2f])
                P.dma("sync", scr_x2.ap()[b, stp * 128:(stp + 1) * 128, :], x2_f,
                      reads=[r_x2f], writes=[r_scrx2])
                for dc in range(NDC):
                    w = min(512, D - dc * 512)
                    pps = (ps_ut, ps_tf)[dc]
                    rps = (r_put, r_ptf)[dc]
                    P.step("tensor",
                           lambda e, pps=pps, dc=dc, w=w, stp=stp:
                           e.matmul(pps[0:1, 0:w], ones_c,
                                    x2_f[:, dc * 512:dc * 512 + w],
                                    start=(stp == 0), stop=(stp == NT - 1),
                                    skip_group_check=True),
                           reads=[r_x2f, r_consts],
                           writes=[rps] if stp == 0 else (),
                           inc=(dc == NDC - 1))
                    if stp == NT - 1:
                        _merge(rps.ready, tM, tM.n)
                _merge(r_x2f.free, tM, tM.n)
                P.step("vector", lambda e: e.tensor_copy(x2b_f, x2_f),
                       reads=[r_x2f], writes=[r_x2b])
                for lo in range(0, ND, 8):
                    n = min(8, ND - lo)
                    for j in range(n):
                        P.step("tensor",
                               lambda e, dt=lo + j, j=j:
                               e.transpose(ps_tr[:, j * 128:(j + 1) * 128],
                                           x2b_f[:, dt * 128:(dt + 1) * 128],
                                           id_bf),
                               reads=[r_x2b, r_consts],
                               writes=[r_ptr] if j == 0 else (),
                               inc=(j == n - 1))
                    _merge(r_ptr.ready, tM, tM.n)
                    P.step("vector",
                           lambda e, lo=lo, n=n, stp=stp:
                           e.tensor_copy(x2T3[:, lo:lo + n,
                                             stp * 128:(stp + 1) * 128],
                                         ps_tr[:, 0:n * 128]),
                           reads=[r_ptr],
                           writes=[r_x2T] if (stp == 0 and lo == 0) else ())
            _merge(r_x2T.ready, vE, vE.n)

            for dc in range(NDC):
                w = min(512, D - dc * 512)
                pps = (ps_ut, ps_tf)[dc]
                rps = (r_put, r_ptf)[dc]
                P.step("vector",
                       lambda e, pps=pps, dc=dc, w=w:
                       e.tensor_scalar(pool_r[0:1, dc * 512:dc * 512 + w],
                                       pps[0:1, 0:w], 1.0 / L, None, ALU.mult),
                       reads=[rps], writes=[r_pool] if dc == 0 else ())
            _merge(r_pool.ready, vE, vE.n)
            for dt in range(ND):
                P.step("tensor",
                       lambda e, dt=dt:
                       e.matmul(ps_ut[:, dt:dt + 1],
                                pool_r[0:1, dt * 128:(dt + 1) * 128],
                                ones_c[0:1, 0:1], start=True, stop=True,
                                skip_group_check=True),
                       reads=[r_pool, r_consts],
                       writes=[r_put] if dt == 0 else (),
                       inc=(dt == ND - 1))
            _merge(r_put.ready, tM, tM.n)
            P.step("vector",
                   lambda e: e.tensor_copy(pool_r[:, 0:ND], ps_ut[:, 0:ND]),
                   reads=[r_put], writes=[r_pool])
            for dt in range(ND):
                P.dma("sync", wrf_sb, wrffn.ap()[dt * 128:(dt + 1) * 128, :],
                      writes=[r_wrf], extra=CE)
                P.step("tensor",
                       lambda e, dt=dt:
                       e.matmul(ps_tf[0:1, 0:E], pool_r[:, dt:dt + 1],
                                wrf_sb, start=(dt == 0), stop=(dt == ND - 1),
                                skip_group_check=True),
                       reads=[r_wrf, r_pool],
                       writes=[r_ptf] if dt == 0 else (),
                       inc=True)
            _merge(r_ptf.ready, tM, tM.n)
            RO = r_rout
            P.step("vector",
                   lambda e: e.tensor_copy(rout[0:1, 0:E], ps_tf[0:1, 0:E]),
                   reads=[r_ptf], writes=[RO])
            P.step("vector",
                   lambda e: e.scalar_tensor_tensor(rout[0:1, 8:8 + E],
                                                    iota_r[0:1, :], -1e-7,
                                                    rout[0:1, 0:E],
                                                    ALU.mult, ALU.add),
                   reads=[RO, r_consts], writes=[RO])
            P.step("vector",
                   lambda e: e.reduce_max(rout[0:1, 16:17], rout[0:1, 8:8 + E],
                                          axis=AX.X),
                   reads=[RO], writes=[RO])
            P.step("vector",
                   lambda e: e.tensor_scalar(rout[0:1, 24:24 + E],
                                             rout[0:1, 8:8 + E],
                                             rout[0:1, 16:17], None, ALU.is_ge),
                   reads=[RO], writes=[RO])
            P.step("vector",
                   lambda e: e.scalar_tensor_tensor(rout[0:1, 32:32 + E],
                                                    rout[0:1, 24:24 + E], -1e30,
                                                    rout[0:1, 8:8 + E],
                                                    ALU.mult, ALU.add),
                   reads=[RO], writes=[RO])
            P.step("vector",
                   lambda e: e.reduce_max(rout[0:1, 17:18], rout[0:1, 32:32 + E],
                                          axis=AX.X),
                   reads=[RO], writes=[RO])
            P.step("vector",
                   lambda e: e.tensor_scalar(rout[0:1, 40:40 + E],
                                             rout[0:1, 32:32 + E],
                                             rout[0:1, 17:18], None, ALU.is_ge),
                   reads=[RO], writes=[RO])
            for dsti, ohoff in ((18, 24), (19, 40)):
                P.step("vector",
                       lambda e, dsti=dsti, ohoff=ohoff:
                       e.scalar_tensor_tensor(rout[0:1, 48:48 + E],
                                              rout[0:1, ohoff:ohoff + E], 1.0,
                                              rout[0:1, 0:E], ALU.mult, ALU.mult,
                                              accum_out=rout[0:1,
                                                             dsti:dsti + 1]),
                       reads=[RO], writes=[RO])
            for dsti, ohoff in ((20, 24), (21, 40)):
                P.step("vector",
                       lambda e, dsti=dsti, ohoff=ohoff:
                       e.scalar_tensor_tensor(rout[0:1, 48:48 + E],
                                              rout[0:1, ohoff:ohoff + E], 1.0,
                                              iota_r[0:1, :], ALU.mult, ALU.mult,
                                              accum_out=rout[0:1,
                                                             dsti:dsti + 1]),
                       reads=[RO, r_consts], writes=[RO])
            P.step("vector",
                   lambda e: e.tensor_sub(rout[0:1, 22:23], rout[0:1, 19:20],
                                          rout[0:1, 18:19]),
                   reads=[RO], writes=[RO])
            P.step("scalar",
                   lambda e: e.activation(rout[0:1, 23:24], rout[0:1, 22:23],
                                          AF.Sigmoid),
                   reads=[RO], writes=[RO])
            P.step("vector",
                   lambda e: e.tensor_scalar(rout[0:1, 26:27], rout[0:1, 23:24],
                                             -1.0, 1.0, ALU.mult, ALU.add),
                   reads=[RO], writes=[RO])
            for col, src in ((0, 26), (1, 23), (2, 20), (3, 21)):
                P.step("tensor",
                       lambda e, col=col, src=src:
                       e.matmul(ps_ut[:, col:col + 1], ones_r[0:1, :],
                                rout[0:1, src:src + 1], start=True, stop=True,
                                skip_group_check=True),
                       reads=[RO, r_consts],
                       writes=[r_put] if col == 0 else (),
                       inc=(col == 3))
            _merge(r_put.ready, tM, tM.n)
            P.step("vector", lambda e: e.tensor_copy(gcol, ps_ut[:, 0:2]),
                   reads=[r_put], writes=[r_gcol])
            P.step("vector", lambda e: e.tensor_copy(ecol, ps_ut[:, 2:4]),
                   reads=[r_put], writes=[r_ecol])
            P.step("vector",
                   lambda e: e.tensor_copy(b2i[0:1, 0:2], rout[0:1, 20:22]),
                   reads=[RO], writes=[r_b2i])
            for k in range(2):
                def b2gather(e, k=k):
                    r1 = greg(e, "gidx")
                    r2 = greg(e, "goff")
                    e.load(r1, b2i[0:1, k:k + 1])
                    e.reg_mul(r2, r1, D)
                    return e.dma_start(out=b2row[k][0:1, :],
                                       in_=AP(b2d, r2, [[D, 1], [1, D]]))
                P.step("gpsimd", b2gather, reads=[r_b2i], writes=[r_b2r[k]],
                       dma=True)
            P.step("vector",
                   lambda e: e.tensor_scalar(b2row[0][0:1, :], b2row[0][0:1, :],
                                             rout[0:1, 26:27], None, ALU.mult),
                   reads=[RO, r_b2r[0]], writes=[r_b2r[0]])
            P.step("vector",
                   lambda e: e.scalar_tensor_tensor(b2row[1][0:1, :],
                                                    b2row[1][0:1, :],
                                                    rout[0:1, 23:24],
                                                    b2row[0][0:1, :],
                                                    ALU.mult, ALU.add),
                   reads=[RO, r_b2r[0], r_b2r[1]], writes=[r_b2r[1]])
            for dt in range(ND):
                P.step("tensor",
                       lambda e, dt=dt:
                       e.matmul(ps_ut[:, 8 + dt:9 + dt],
                                b2row[1][0:1, dt * 128:(dt + 1) * 128],
                                ones_c[0:1, 0:1], start=True, stop=True,
                                skip_group_check=True),
                       reads=[r_b2r[1], r_consts],
                       writes=[r_put] if dt == 0 else (),
                       inc=(dt == ND - 1))
            _merge(r_put.ready, tM, tM.n)
            P.step("vector",
                   lambda e: e.tensor_copy(b2cols, ps_ut[:, 8:8 + ND]),
                   reads=[r_put], writes=[r_b2c])

            for k in range(2 if cut >= 5 else 0):
                def w1load(e, k=k):
                    r1 = greg(e, "gidx")
                    r2 = greg(e, "goff")
                    e.load(r1, b2i[0:1, k:k + 1])
                    e.reg_mul(r2, r1, D * FF)
                    src_ = AP(w1d, r2, [[FF, 128], [128 * FF, ND], [1, FF]])
                    return e.dma_start(out=w1b3[:, :, :], in_=src_)
                P.step("gpsimd", w1load, reads=[r_b2i],
                       writes=[r_w1b], dma=True)
                def w2cast(e, k=k):
                    r1 = greg(e, "gidx")
                    r2 = greg(e, "goff")
                    e.load(r1, b2i[0:1, k:k + 1])
                    e.reg_mul(r2, r1, FF * D)
                    return e.dma_start(out=AP(scr_w2[k], 0, [[1, FF * D]]),
                                       in_=AP(w2d, r2, [[1, FF * D]]))
                P.step("gpsimd", w2cast, reads=[r_b2i],
                       writes=[r_scrw2[k]], dma=True)
                P.step("vector",
                       lambda e, k=k: e.tensor_copy(b2i[0:1, 0:1],
                                                    rout[0:1, 20 + k:21 + k]),
                       reads=[RO], writes=[r_b2i])
                def b1gather(e):
                    r1 = greg(e, "gidx")
                    r2 = greg(e, "goff")
                    e.load(r1, b2i[0:1, 0:1])
                    e.reg_mul(r2, r1, FF)
                    return e.dma_start(out=b1row[0:1, :],
                                       in_=AP(b1d, r2, [[FF, 1], [1, FF]]))
                P.step("gpsimd", b1gather, reads=[r_b2i], writes=[r_b1r],
                       dma=True)
                for ft in range(NF):
                    P.step("tensor",
                           lambda e, ft=ft:
                           e.matmul(ps_ut[:, ft:ft + 1],
                                    b1row[0:1, ft * 128:(ft + 1) * 128],
                                    ones_c[0:1, 0:1], start=True, stop=True,
                                    skip_group_check=True),
                           reads=[r_b1r, r_consts],
                           writes=[r_put] if ft == 0 else (),
                           inc=(ft == NF - 1))
                _merge(r_put.ready, tM, tM.n)
                P.step("vector",
                       lambda e: e.tensor_copy(b1cols, ps_ut[:, 0:NF]),
                       reads=[r_put], writes=[r_b1c])
                for sp in range(NSTRIP):
                    for fpo in range(NF):
                        sl = fpo % 2
                        for dt in range(ND):
                            P.step("tensor",
                                   lambda e, ps=ps_sc[sl], fpo=fpo, dt=dt, sp=sp:
                                   e.matmul(ps[:, 0:STRIP],
                                            w1b3[:, dt,
                                                 fpo * 128:(fpo + 1) * 128],
                                            x2T3[:, dt,
                                                 sp * STRIP:(sp + 1) * STRIP],
                                            start=(dt == 0), stop=(dt == ND - 1)),
                                   reads=[r_w1b, r_x2T],
                                   writes=[r_ps[sl]] if dt == 0 else (),
                                   inc=(dt == ND - 1))
                        _merge(r_ps[sl].ready, tM, tM.n)
                        P.step("scalar",
                               lambda e, ps=ps_sc[sl], fpo=fpo:
                               e.activation(hT3[:, fpo, :], ps[:, 0:STRIP],
                                            AF.Gelu, bias=b1cols[:, fpo:fpo + 1]),
                               reads=[r_ps[sl], r_b1c],
                               writes=[r_hT] if fpo == 0 else ())
                    _merge(r_hT.ready, sA, sA.n)
                    for dpo in range(ND):
                        wc, rwc = w2c3[dpo % 2], r_w2c[dpo % 2]
                        P.dma("sync", wc,
                              AP(scr_w2[k], dpo * 128,
                                 [[D, 128], [128 * D, NF], [1, 128]]),
                              reads=[r_scrw2[k]], writes=[rwc])
                        yoff = 512 * (dpo % 2)
                        for ft in range(NF):
                            P.step("tensor",
                                   lambda e, yoff=yoff, wc=wc, ft=ft:
                                   e.matmul(ps_big[:, yoff:yoff + STRIP],
                                            wc[:, ft, :], hT3[:, ft, :],
                                            start=(ft == 0), stop=(ft == NF - 1)),
                                   reads=[rwc, r_hT],
                                   writes=[r_big] if ft == 0 else (),
                                   inc=(ft == NF - 1))
                        _merge(r_big.ready, tM, tM.n)
                        if k == 0:
                            P.step("vector",
                                   lambda e, yoff=yoff, dpo=dpo:
                                   e.tensor_scalar(yev[:, 0:STRIP],
                                                   ps_big[:, yoff:yoff + STRIP],
                                                   gcol[:, 0:1],
                                                   b2cols[:, dpo:dpo + 1],
                                                   ALU.mult, ALU.add),
                                   reads=[r_big, r_b2c, r_gcol],
                                   writes=[r_yev])
                            _merge(r_yev.ready, vE, vE.n)
                            P.dma("sync",
                                  AP(scr_ffn, (dpo * 128) * L + sp * STRIP,
                                     [[L, 128], [1, STRIP]]),
                                  yev, reads=[r_yev], writes=[r_scrf])
                        else:
                            P.step("vector",
                                   lambda e, yoff=yoff, dpo=dpo:
                                   e.tensor_scalar(yev[:, 0:STRIP],
                                                   ps_big[:, yoff:yoff + STRIP],
                                                   gcol[:, 1:2], None,
                                                   ALU.mult),
                                   reads=[r_big, r_gcol],
                                   writes=[r_yev])
                            _merge(r_yev.ready, vE, vE.n)
                            P.dma("gpsimd",
                                  AP(scr_ffn, (dpo * 128) * L + sp * STRIP,
                                     [[L, 128], [1, STRIP]]),
                                  yev, reads=[r_yev, r_scrf],
                                  writes=[r_scrf], accum_op=ALU.add)

            for kb in range(NT if cut >= 6 else 0):
                P.dma("sync", yT3,
                      AP(scr_ffn, kb * 128, [[L, 128], [128 * L, ND], [1, 128]]),
                      reads=[r_scrf], writes=[r_yT])
                P.dma("sync", x_f[0],
                      scr_x2.ap()[b, kb * 128:(kb + 1) * 128, :],
                      reads=[r_scrx2], writes=[r_xf[0]])
                for lo in range(0, ND, 4):
                    n = min(4, ND - lo)
                    for j in range(n):
                        P.step("tensor",
                               lambda e, dt=lo + j, j=j:
                               e.transpose(ps_tf[:, j * 128:(j + 1) * 128],
                                           yT_f[:, dt * 128:(dt + 1) * 128],
                                           id_f),
                               reads=[r_yT, r_consts],
                               writes=[r_ptf] if j == 0 else (),
                               inc=(j == n - 1))
                    _merge(r_ptf.ready, tM, tM.n)
                    P.step("vector",
                           lambda e, lo=lo, n=n:
                           e.tensor_copy(fn_f[:, lo * 128:(lo + n) * 128],
                                         ps_tf[:, 0:n * 128]),
                           reads=[r_ptf],
                           writes=[r_fn] if lo == 0 else ())
                P.step("vector", lambda e: e.tensor_add(t2_f, fn_f, x_f[0]),
                       reads=[r_fn, r_xf[0]], writes=[r_t2])
                P.step("scalar",
                       lambda e: e.activation(sq_f, t2_f, AF.Square,
                                              accum_out=stat[:, 1:2]),
                       reads=[r_t2, r_stat], writes=[r_sq, r_stat])
                P.step("vector",
                       lambda e: e.reduce_sum(stat[:, 0:1], t2_f, axis=AX.X),
                       reads=[r_t2, r_stat], writes=[r_stat])
                P.step("vector",
                       lambda e: e.tensor_scalar(stat[:, 2:3], stat[:, 0:1],
                                                 1.0 / D, None, ALU.mult),
                       reads=[r_stat], writes=[r_stat])
                P.step("vector",
                       lambda e: e.scalar_tensor_tensor(stat[:, 4:5],
                                                        stat[:, 2:3], 1.0,
                                                        stat[:, 2:3],
                                                        ALU.mult, ALU.mult),
                       reads=[r_stat], writes=[r_stat])
                P.step("vector",
                       lambda e: e.scalar_tensor_tensor(stat[:, 5:6],
                                                        stat[:, 1:2], 1.0 / D,
                                                        stat[:, 4:5],
                                                        ALU.mult, ALU.subtract),
                       reads=[r_stat], writes=[r_stat])
                P.step("scalar",
                       lambda e: e.activation(stat[:, 6:7], stat[:, 5:6],
                                              AF.Sqrt, bias=eps_sb),
                       reads=[r_stat, r_consts], writes=[r_stat])
                P.step("vector",
                       lambda e: e.reciprocal(stat[:, 7:8], stat[:, 6:7]),
                       reads=[r_stat], writes=[r_stat])
                P.step("vector",
                       lambda e: e.tensor_scalar(o_f, t2_f, stat[:, 2:3],
                                                 stat[:, 7:8], ALU.subtract,
                                                 ALU.mult),
                       reads=[r_t2, r_stat], writes=[r_of])
                P.dma("sync", yout.ap()[b, kb * 128:(kb + 1) * 128, :], o_f,
                      reads=[r_of], writes=[r_y])
            P.fence()

        if cut < 6:
            dummy_out()
        allsems = [(s, s.n) for s in P.dma_sems]
        P.step("sync", lambda e: e.nop(), extra=allsems, inc=False)
        P.step("gpsimd", lambda e: e.nop(), extra=allsems, inc=False)

        stats = dict(nops=P.nops, per_engine={e: len(P.items[e])
                                              for e in ENGINES},
                     pers=PERS_END, attn=ATTN_END, ffn=FFN_END)
        P.replay()
    nc.finalize()
    return nc, stats


def host_prep(cfg, x, Wq, Wk, Wv, Wr, u_bias, v_bias, fWr, fW1, fb1, fW2, fb2):
    B = cfg["BPC"]
    L, D, H, FF, E = (cfg[k] for k in ("L", "D", "H", "FF", "E"))
    Lr = 2 * L - 1
    NG = H // 2
    NB = x.shape[0]
    ncores = NB // B
    inv_freq = 1.0 / (10000.0 ** (np.arange(0, D, 2, dtype=np.float32) /
                                  np.float32(D)))
    pos = np.arange(-(L - 1), L, dtype=np.float32)
    ang = pos[:, None] * inv_freq[None, :]
    R = np.zeros((Lr, D), dtype=np.float32)
    R[:, 0::2] = np.sin(ang)
    R[:, 1::2] = np.cos(ang)
    common = {
        "wqT": np.ascontiguousarray(np.asarray(Wq).T, np.float32),
        "wkT": np.ascontiguousarray(np.asarray(Wk).T, np.float32),
        "wvT": np.ascontiguousarray(np.asarray(Wv).T, np.float32),
        "wrT": np.ascontiguousarray(np.asarray(Wr).T, np.float32),
        "rT": np.ascontiguousarray(R.T, np.float32),
        "ucols": np.ascontiguousarray(
            np.asarray(u_bias, np.float32).reshape(-1).reshape(NG, 128).T),
        "vcols": np.ascontiguousarray(
            np.asarray(v_bias, np.float32).reshape(-1).reshape(NG, 128).T),
        "wrffn": np.ascontiguousarray(np.asarray(fWr), np.float32),
        "w1": np.ascontiguousarray(np.asarray(fW1).reshape(E * D, FF),
                                   np.float32),
        "b1": np.ascontiguousarray(np.asarray(fb1), np.float32),
        "w2": np.ascontiguousarray(np.asarray(fW2).reshape(E * FF, D),
                                   np.float32),
        "b2": np.ascontiguousarray(np.asarray(fb2), np.float32),
    }
    xx = np.ascontiguousarray(np.asarray(x), np.float32)
    in_maps = []
    for c in range(ncores):
        m = dict(common)
        m["xin"] = np.ascontiguousarray(xx[c * B:(c + 1) * B])
        in_maps.append(m)
    return in_maps, ncores


def np_ref(cfg, x, Wq, Wk, Wv, Wr, u_bias, v_bias, fWr, fW1, fb1, fW2, fb2):
    from scipy.special import erf
    L, D, H, FF, E = (cfg[k] for k in ("L", "D", "H", "FF", "E"))
    DH = 64
    Lr = 2 * L - 1
    NB = x.shape[0]
    scale = 1.0 / np.sqrt(np.float32(DH))
    inv_freq = 1.0 / (10000.0 ** (np.arange(0, D, 2, dtype=np.float32) /
                                  np.float32(D)))
    pos = np.arange(-(L - 1), L, dtype=np.float32)
    ang = pos[:, None] * inv_freq[None, :]
    R = np.zeros((Lr, D), dtype=np.float32)
    R[:, 0::2] = np.sin(ang)
    R[:, 1::2] = np.cos(ang)

    def relshift(z):
        Hh, Lq, Lrr = z.shape
        zp = np.concatenate([np.zeros((Hh, Lq, 1), z.dtype), z], axis=-1)
        return zp.reshape(Hh, Lrr + 1, Lq)[:, 1:].reshape(Hh, Lq, Lrr)[..., Lrr - Lq:]

    def layernorm(v):
        m = v.mean(-1, keepdims=True)
        var = v.var(-1, keepdims=True)
        return (v - m) / np.sqrt(var + 1e-5)

    out = np.zeros_like(x)
    rp = (R @ Wr.T).reshape(Lr, H, DH).transpose(1, 0, 2)
    for bi in range(NB):
        w = x[bi]
        q = (w @ Wq.T).reshape(L, H, DH).transpose(1, 0, 2)
        kk = (w @ Wk.T).reshape(L, H, DH).transpose(1, 0, 2)
        vv = (w @ Wv.T).reshape(L, H, DH).transpose(1, 0, 2)
        AC = np.einsum('hid,hjd->hij', q + u_bias[:, None, :], kk)
        BD = np.einsum('hid,hjd->hij', q + v_bias[:, None, :], rp)
        BD = relshift(BD)
        logits = (AC + BD) * scale
        p = np.exp(logits - logits.max(-1, keepdims=True))
        p = p / p.sum(-1, keepdims=True)
        o = np.einsum('hij,hjd->hid', p, vv).transpose(1, 0, 2).reshape(L, D)
        x2 = layernorm(w + o)
        pooled = x2.mean(axis=0)
        logitsr = pooled @ fWr
        order = np.argsort(-logitsr, kind="stable")
        e0, e1 = order[0], order[1]
        s0, s1 = logitsr[e0], logitsr[e1]
        ww = np.exp([s0, s1] - max(s0, s1))
        ww = ww / ww.sum()
        y = np.zeros((L, D), np.float32)
        for g, ei in ((ww[0], e0), (ww[1], e1)):
            h = x2 @ fW1[ei] + fb1[ei]
            h = 0.5 * h * (1 + erf(h / np.sqrt(2.0)))
            y += g * (h @ fW2[ei] + fb2[ei])
        out[bi] = layernorm(x2 + y)
    return out


_CACHE = {}


def kernel(L0_attn_w, L0_attn_Wq, L0_attn_Wk, L0_attn_Wv, L0_attn_Wr,
           L0_attn_u_bias, L0_attn_v_bias, L0_ffn_Wr, L0_ffn_W1, L0_ffn_b1,
           L0_ffn_W2, L0_ffn_b2):
    from concourse.bass_utils import run_bass_kernel_spmd
    cfg = dict(FULL)
    if "nc" not in _CACHE:
        _CACHE["nc"] = build(cfg)[0]
    nc = _CACHE["nc"]
    in_maps, ncores = host_prep(cfg, np.asarray(L0_attn_w), L0_attn_Wq,
                                L0_attn_Wk, L0_attn_Wv, L0_attn_Wr,
                                L0_attn_u_bias, L0_attn_v_bias, L0_ffn_Wr,
                                L0_ffn_W1, L0_ffn_b1, L0_ffn_W2, L0_ffn_b2)
    assert ncores == 8
    res = run_bass_kernel_spmd(nc, in_maps, core_ids=list(range(8)))
    out = np.concatenate([np.asarray(res.results[c]["yout"])
                          for c in range(8)], axis=0)
    return out.astype(np.float32, copy=False)
